# version 18
# baseline (speedup 1.0000x reference)
import math
import numpy as np
import ml_dtypes
from contextlib import ExitStack

import concourse.bass as bass
import concourse.mybir as mybir
from concourse.bass_utils import run_bass_kernel_spmd

F32 = mybir.dt.float32
BF16 = mybir.dt.bfloat16
AF = mybir.ActivationFunctionType
ALU = mybir.AluOpType
AX = mybir.AxisListType
BF = ml_dtypes.bfloat16

NCORES = 8
B = 4
S = 2048
D = 2048
SL = S // NCORES
NT = B * SL
HW = 512
KTB = HW + SL
HD = 128
NE = 8
NEG = -1.0e30
EPS = 1e-5
DMA_CAST_MAX = 8192


def _f(a):
    return np.ascontiguousarray(a, dtype=np.float32)


def _rope_tables(positions, scale=1.0):
    pos = np.asarray(positions, dtype=np.float64)
    inv = 1.0 / (10000.0 ** (np.arange(0, HD, 2, dtype=np.float64) / HD))
    ang = np.concatenate([inv, inv])[:, None] * pos[None, :]
    out = []
    for t in (np.cos(ang) * scale, np.sin(ang) * scale):
        hi = t.astype(np.float32)
        lo = (t - hi.astype(np.float64)).astype(np.float32)
        out += [hi, lo]
    return out


def _prep_shared(inp):
    sh = {}
    r4 = lambda w, K, N: np.ascontiguousarray(
        np.asarray(w, dtype=np.float32).reshape(K // 128, 128, N // 128, 128)
        .transpose(2, 1, 0, 3))
    sh["wq"] = r4(inp["L0_attn_Wq"], 2048, 2048)
    wqf = np.asarray(inp["L0_attn_Wq"], np.float32)
    perm = np.concatenate([np.arange(64, 128), np.arange(0, 64)])
    wq_p = wqf.reshape(2048, 16, 128)[:, :, perm].reshape(2048, 2048)
    sh["wqp"] = r4(wq_p, 2048, 2048)
    sh["wk"] = r4(inp["L0_attn_Wk"], 2048, 512)
    sh["wvr"] = _f(np.asarray(inp["L0_attn_Wv"], np.float32)
                   .reshape(16, 128, 512).transpose(1, 0, 2))
    sh["wo"] = r4(inp["L0_attn_Wo"], 2048, 2048)
    sh["wqkv"] = r4(np.asarray(inp["L0_ffn_qkv_weight"], np.float32).T, 2048, 3072)
    sh["wproj"] = r4(np.asarray(inp["L0_ffn_proj_weight"], np.float32).T, 2048, 2048)
    sh["wrout"] = _f(np.asarray(inp["L0_ffn_router_weight"], np.float32)
                     .reshape(16, 128, 8).transpose(1, 0, 2).reshape(128, 128))
    w1 = np.asarray(inp["L0_ffn_moe_w1"], np.float32)
    sh["w1"] = np.ascontiguousarray(
        w1.reshape(NE, 16, 128, 32, 128).transpose(0, 3, 2, 1, 4))
    w2 = np.asarray(inp["L0_ffn_moe_w2"], np.float32)
    sh["w2"] = np.ascontiguousarray(
        w2.reshape(NE, 2, 16, 128, 16, 128).transpose(0, 1, 4, 3, 2, 5))
    for nm, key in [("l1w", "L0_ffn_ln1_weight"), ("l1b", "L0_ffn_ln1_bias"),
                    ("l2w", "L0_ffn_ln2_weight"), ("l2b", "L0_ffn_ln2_bias")]:
        sh[nm] = _f(np.asarray(inp[key], np.float32).reshape(16, 128).T)
    sh["lam"] = _f(np.asarray(inp["L0_attn_lam"], np.float32).reshape(1, 16))
    sh["identb"] = np.ascontiguousarray(np.eye(128, dtype=np.float32).astype(BF))
    rot = np.zeros((128, 128), dtype=np.float32)
    for m in range(64):
        rot[m + 64, m] = -1.0
        rot[m, m + 64] = 1.0
    sh["rotb"] = np.ascontiguousarray(rot.astype(BF))
    sh["identf"] = _f(np.eye(128, dtype=np.float32))
    oh = np.zeros((8, 1024), dtype=np.float32)
    for e in range(8):
        oh[e, e * 128:(e + 1) * 128] = 1.0
    sh["oneh8"] = _f(oh)
    sh["onesf"] = _f(np.ones((1, 128), dtype=np.float32))
    sh["onesb"] = np.ascontiguousarray(np.ones((128, 1), np.float32).astype(BF))
    return sh


def _prep_core(core, x, shared):
    s0 = SL * core
    d = dict(shared)
    xr = np.asarray(x[:, s0:s0 + SL, :], dtype=np.float32)
    d["x"] = _f(xr.transpose(2, 0, 1).reshape(16, 128, NT))
    xh = np.zeros((B, HW, D), dtype=np.float32)
    lo = max(0, s0 - HW)
    n = s0 - lo
    if n > 0:
        xh[:, HW - n:, :] = x[:, lo:s0, :]
    d["xh"] = _f(xh.reshape(B, 2, 256, D).transpose(0, 1, 3, 2)
                 .reshape(B * 2, 16, 128, 256))
    kpos = np.maximum(0, np.arange(s0 - HW, s0 + SL))
    qpos = np.tile(np.arange(s0, s0 + SL), B)
    ch, cl, sh_, sl_ = _rope_tables(kpos)
    d["ckh"], d["ckl"], d["skh"], d["skl"] = _f(ch), _f(cl), _f(sh_), _f(sl_)
    ch, cl, sh_, sl_ = _rope_tables(qpos, scale=1.0 / math.sqrt(HD // 2))
    sgn = np.concatenate([-np.ones(64), np.ones(64)]).astype(np.float32)[:, None]
    d["cqh"], d["cql"] = _f(ch), _f(cl)
    d["sqh"], d["sql"] = _f(sh_ * sgn), _f(sl_ * sgn)
    masks = np.zeros((2, 128, 640), dtype=np.float32)
    i = np.arange(128)[:, None]
    j = np.arange(640)[None, :]
    for qt in range(2):
        ok = (j - i >= 1) & (j - i <= HW) & (s0 + 128 * qt - HW + j >= 0)
        masks[qt] = np.where(ok, 0.0, NEG)
    d["mask"] = _f(masks)
    return d



class Ctx:
    def __init__(self, nc, es):
        self.nc = nc
        self.sem = {}
        self.cnt = {}
        for name in ("dW", "dW0", "dW1", "dW2", "dW3", "dW4", "dA", "dO",
                     "dO0", "dO1", "sT", "sV", "sS", "sG"):
            self.sem[name] = es.enter_context(nc.semaphore(name))
            self.cnt[name] = 0

    def inc(self, name, instr, n=None):
        n = n if n is not None else (16 if name.startswith("d") else 1)
        instr.then_inc(self.sem[name], n)
        self.cnt[name] += n
        return self.cnt[name]

    def wait(self, eng, *names, **kw):
        for name in names:
            v = kw.get("val", self.cnt[name])
            if v > 0:
                eng.wait_ge(self.sem[name], v)


def view(t, dtype, p0, np_, byte0, ncols):
    esz = mybir.dt.size(dtype)
    assert byte0 % esz == 0
    return t.bitcast(dtype)[p0:p0 + np_, byte0 // esz: byte0 // esz + ncols]


DBG = None


def build_nc():
    nc = bass.Bass("TRN2", target_bir_lowering=False)
    I = {}
    def di(name, shape, dt=F32):
        I[name] = nc.dram_tensor(name, list(shape), dt, kind="ExternalInput")
    di("x", (16, 128, NT)); di("xh", (8, 16, 128, 256))
    for nm in ("ckh", "ckl", "skh", "skl"):
        di(nm, (128, KTB))
    for nm in ("cqh", "cql", "sqh", "sql"):
        di(nm, (128, NT))
    di("mask", (2, 128, 640))
    di("wq", (16, 128, 16, 128)); di("wqp", (16, 128, 16, 128))
    di("wk", (4, 128, 16, 128)); di("wvr", (128, 16, 512))
    di("wo", (16, 128, 16, 128)); di("wqkv", (24, 128, 16, 128)); di("wproj", (16, 128, 16, 128))
    di("wrout", (128, 128))
    di("w1", (NE, 32, 128, 16, 128)); di("w2", (NE, 2, 16, 128, 16, 128))
    di("l1w", (128, 16)); di("l1b", (128, 16)); di("l2w", (128, 16)); di("l2b", (128, 16))
    di("lam", (1, 16))
    di("identb", (128, 128), BF16); di("identf", (128, 128))
    di("rotb", (128, 128), BF16)
    di("oneh8", (8, 1024)); di("onesf", (1, 128)); di("onesb", (128, 1), BF16)
    out = nc.dram_tensor("out", [16, 128, NT], F32, kind="ExternalOutput")

    s_xn16 = nc.dram_tensor("s_xn16", [4, 128, 16 * 256], BF16)
    s_k = nc.dram_tensor("s_k", [128, 4 * 3072], BF16)
    s_v = nc.dram_tensor("s_v", [128, 24 * 512], BF16)
    s_g = nc.dram_tensor("s_g", [16, 128, NT], F32)

    dbg = {}
    _dbg_specs = {
        "p1": [("dxn32", [128, 16 * NT], F32), ("dxn16", [4, 128, 16 * 256], BF16),
               ("dk", [128, 4 * 3072], BF16), ("dv", [128, 24 * 512], BF16),
               ("dl1s", [128, 2048], F32)],
        "p2a": [("dln1", [128, 16 * NT], BF16), ("dqkv", [128, 24 * NT], BF16)],
        "p2b": [("dao2", [128, 16 * NT], BF16)],
        "p2c": [("dh2", [128, 16 * NT], F32), ("dxmoe", [128, 16 * NT], BF16),
                ("dgfm", [8, NT], F32)],
        "p3": [("dg", [16, 128, NT], F32)],
    }
    if DBG in _dbg_specs:
        for nm, shape, dt in _dbg_specs[DBG]:
            dbg[nm] = nc.dram_tensor(nm, shape, dt, kind="ExternalOutput")

    es = ExitStack()
    with es:
        C = Ctx(nc, es)
        sbt = lambda n, s, d: es.enter_context(nc.sbuf_tensor(n, s, d))
        AR_A = sbt("AR_A", [128, 44 * 1024], mybir.dt.uint8)
        AR_B = sbt("AR_B", [128, 64 * 1024], mybir.dt.uint8)
        AR_C = sbt("AR_C", [128, 48 * 1024], mybir.dt.uint8)
        AR_D = sbt("AR_D", [128, 32 * 1024], mybir.dt.uint8)
        identb = sbt("c_identb", [128, 128], BF16)
        rotb = sbt("c_rotb", [128, 128], BF16)
        identf = sbt("c_identf", [128, 128], F32)
        onesf = sbt("c_onesf", [1, 128], F32)
        onesb = sbt("c_onesb", [128, 1], BF16)
        lamb = sbt("lamb", [128, 16], F32)
        l1w = sbt("c_l1w", [128, 16], F32); l1b = sbt("c_l1b", [128, 16], F32)
        l2w = sbt("c_l2w", [128, 16], F32); l2b = sbt("c_l2b", [128, 16], F32)
        wroutb = sbt("wroutb", [128, 128], BF16)
        lam1 = sbt("lam1", [1, 16], F32)
        epsb = sbt("epsb", [128, 1], F32)
        R = sbt("R", [128, 1536], F32)
        R2 = sbt("R2", [128, 2048], F32)

        def v_cosk(p0, np_, c0, nn):
            return view(AR_D, F32, p0, np_, 24 * 1024 + c0 * 4, nn)

        def v_sink(p0, np_, c0, nn):
            return view(AR_D, F32, p0, np_, 27 * 1024 + 72 + c0 * 4, nn)

        with nc.Block() as blk:
            @blk.sync
            def _(sy):
                for dst, src in [(identb, I["identb"]), (rotb, I["rotb"]),
                                 (identf, I["identf"]),
                                 (onesf, I["onesf"]), (onesb, I["onesb"]),
                                 (l1w, I["l1w"]), (l1b, I["l1b"]),
                                 (l2w, I["l2w"]), (l2b, I["l2b"]),
                                 (lam1, I["lam"])]:
                    C.inc("dA", sy.dma_start(dst[:, :], src[:, :]))
                C.inc("dA", sy.dma_start(v_cosk(0, 128, 0, KTB), I["ckh"][:, :]))
                C.inc("dA", sy.dma_start(v_sink(0, 128, 0, KTB), I["skh"][:, :]))
                C.inc("dA", sy.dma_start(view(AR_A, F32, 0, 128, 0, KTB), I["ckl"][:, :]))
                C.inc("dA", sy.dma_start(view(AR_A, F32, 0, 128, 4 * KTB, KTB), I["skl"][:, :]))

            @blk.gpsimd
            def _(gp):
                C.inc("dW", gp.dma_start(wroutb[:, :], I["wrout"][:, :],
                                         max_dma_last_dim=DMA_CAST_MAX))

            @blk.vector
            def _(ve):
                ve.memset(epsb[:, :], EPS)
                ve.memset(R[:, :], 0.0)
                ve.memset(R2[:, :], 0.0)
                C.wait(ve, "dA")
                ve.tensor_tensor(v_cosk(0, 128, 0, KTB), v_cosk(0, 128, 0, KTB),
                                 view(AR_A, F32, 0, 128, 0, KTB), op=ALU.add)
                ve.tensor_tensor(v_sink(0, 128, 0, KTB), v_sink(0, 128, 0, KTB),
                                 view(AR_A, F32, 0, 128, 4 * KTB, KTB), op=ALU.add)

        with nc.Block() as blk, nc.psum_tensor("pb0", [128, 512], F32) as pb0:
            @blk.tensor
            def _(te):
                C.wait(te, "dA")
                C.inc("sT", te.matmul(pb0[:, 0:16], onesf[:, :], lam1[:, :],
                                      start=True, stop=True))

            @blk.scalar
            def _(sc):
                C.wait(sc, "sT")
                C.inc("sS", sc.copy(lamb[:, :], pb0[:, 0:16]))

        def v_xnslab(c, t0=0, nn=256):
            return view(AR_A, BF16, 0, 128, (c * 256 + t0) * 2, nn)

        def v_xsl(c, t0=0, nn=256):
            return view(AR_A, F32, 0, 128, 8 * 1024 + (c * 256 + t0) * 4, nn)

        def v_xbuf(c, nn=256):
            return view(AR_A, BF16, 0, 128, 24 * 1024 + c * 512, nn)

        def v_sqb(c, nn=256):
            return view(AR_A, BF16, 0, 128, 32 * 1024 + c * 512, nn)

        def v_tmp(slot, p0=0, np_=128, nn=256):
            return view(AR_A, F32, p0, np_, 40 * 1024 + slot * 1024, nn)

        def v_xn32(c, t0=0, nn=NT):
            return view(AR_B, F32, 0, 128, (c * NT + t0) * 4, nn)

        def v_ks(g, k0, nn, p0=0, np_=128):
            return view(AR_C, BF16, p0, np_, (g * 3072 + k0) * 2, nn)

        def v_vs(tile, c0=0, nn=512):
            return view(AR_C, BF16, 0, 128, 24 * 1024 + (tile * 512 + c0) * 2, nn)

        def v_wvb(kc):
            return view(AR_D, BF16, 0, 128, kc * 1024, 512)

        def v_wkb(par, kc):
            return view(AR_D, BF16, 0, 128, 16 * 1024 + par * 4096 + kc * 256, 128)

        slabs = [("h", hb) for hb in range(8)] + [("m", b) for b in range(4)]

        with nc.Block() as blk, \
             nc.psum_tensor("p_s1", [1, 256], F32) as p_s1, \
             nc.psum_tensor("p_s2", [1, 256], F32) as p_s2, \
             nc.psum_tensor("p_bc", [128, 512], F32) as p_bc, \
             nc.psum_tensor("p_kv", [128, 512], F32) as p_kv, \
             nc.psum_tensor("p_kv2", [128, 512], F32) as p_kv2, \
             nc.psum_tensor("p_rot", [128, 256], F32) as p_rot, \
             nc.psum_tensor("p_rot2", [128, 256], F32) as p_rot2:

            @blk.gpsimd
            def _(gp):
                C.inc("dW", gp.dma_start(view(AR_D, BF16, 0, 128, 0, 16 * 512),
                                         I["wvr"].rearrange("p a b -> p (a b)"),
                                         max_dma_last_dim=DMA_CAST_MAX))

            sv_applydone = [0] * len(slabs)
            ss_lastread_xsl = [0] * len(slabs)

            for si, (kind, idx) in enumerate(slabs):
                main = kind == "m"
                b = idx // 2 if kind == "h" else idx
                half = idx % 2 if kind == "h" else None
                o0 = half * 256 if kind == "h" else 512
                kcol0 = b * 768 + o0
                vt0 = b * 6 + (half * 2 if kind == "h" else 4)

                @blk.sync
                def _(sy, si=si, kind=kind, idx=idx):
                    if si >= 1:
                        sy.wait_ge(C.sem["sV"], sv_applydone[si - 1])
                        sy.wait_ge(C.sem["sS"], ss_lastread_xsl[si - 1])
                    for c in range(16):
                        if kind == "h":
                            C.inc("dA", sy.dma_start(v_xsl(c), I["xh"][idx, c, :, :]))
                        else:
                            C.inc("dA", sy.dma_start(
                                v_xsl(c), I["x"][c, :, idx * 256:(idx + 1) * 256]))

                @blk.vector
                def _(ve):
                    C.wait(ve, "dA")
                    for c in range(15):
                        ve.tensor_copy(v_xbuf(c), v_xsl(c))
                    C.inc("sV", ve.tensor_copy(v_xbuf(15), v_xsl(15)))

                @blk.scalar
                def _(sc):
                    C.wait(sc, "dA")
                    for c in range(15):
                        sc.activation(v_sqb(c), v_xsl(c), AF.Square)
                    C.inc("sS", sc.activation(v_sqb(15), v_xsl(15), AF.Square))

                @blk.tensor
                def _(te):
                    C.wait(te, "sV", "sS")
                    for c in range(16):
                        te.matmul(p_s1[:, :], onesb[:, :], v_xbuf(c),
                                  start=(c == 0), stop=(c == 15))
                    for c in range(15):
                        te.matmul(p_s2[:, :], onesb[:, :], v_sqb(c),
                                  start=(c == 0), stop=False)
                    C.inc("sT", te.matmul(p_s2[:, :], onesb[:, :], v_sqb(15),
                                          start=False, stop=True))

                @blk.vector
                def _(ve):
                    C.wait(ve, "sT")
                    ve.tensor_scalar(R[0:1, 0:256], p_s1[:, :], 1.0 / D, None,
                                     op0=ALU.mult)
                    ve.drain()
                    ve.tensor_tensor(R[0:1, 768:1024], R[0:1, 0:256], R[0:1, 0:256],
                                     op=ALU.mult)
                    ve.drain()
                    C.inc("sV", ve.scalar_tensor_tensor(
                        R[0:1, 768:1024], p_s2[:, :], 1.0 / D, R[0:1, 768:1024],
                        op0=ALU.mult, op1=ALU.subtract))

                @blk.scalar
                def _(sc):
                    C.wait(sc, "sV")
                    sc.activation(R[0:1, 256:512], R[0:1, 768:1024], AF.Ln,
                                  bias=epsb[0:1, :])
                    sc.drain()
                    C.inc("sS", sc.activation(R[0:1, 256:512], R[0:1, 256:512], AF.Exp,
                                              scale=-0.5))

                @blk.vector
                def _(ve):
                    C.wait(ve, "sS")
                    C.inc("sV", ve.scalar_tensor_tensor(
                        R[0:1, 512:768], R[0:1, 0:256], -1.0, R[0:1, 256:512],
                        op0=ALU.mult, op1=ALU.mult))

                @blk.tensor
                def _(te):
                    C.wait(te, "sV", "sS")
                    te.matmul(p_bc[:, 0:256], onesf[:, :], R[0:1, 256:512],
                              start=True, stop=True)
                    C.inc("sT", te.matmul(p_bc[:, 256:512], onesf[:, :],
                                          R[0:1, 512:768], start=True, stop=True))

                @blk.vector
                def _(ve, main=main, b=b, si=si):
                    C.wait(ve, "sT", "dA")
                    for c in range(16):
                        ve.tensor_tensor(v_xsl(c), v_xsl(c), p_bc[:, 0:256], op=ALU.mult)
                    ve.drain()
                    for c in range(16):
                        dst = v_xn32(c, b * 256, 256) if main else v_xnslab(c)
                        ins = ve.tensor_tensor(dst, v_xsl(c), p_bc[:, 256:512], op=ALU.add)
                        if c == 15:
                            sv_applydone[si] = C.inc("sV", ins)

                if main:
                    @blk.scalar
                    def _(sc, b=b, si=si):
                        C.wait(sc, "sV")
                        for c in range(16):
                            sc.copy(v_xnslab(c), v_xn32(c, b * 256, 256))
                        for c in range(15):
                            sc.activation(v_sqb(c), v_xn32(c, b * 256, 256), AF.Square)
                        ss_lastread_xsl[si] = C.inc(
                            "sS", sc.activation(v_sqb(15), v_xn32(15, b * 256, 256),
                                                AF.Square))
                else:
                    ss_lastread_xsl[si] = C.cnt["sS"]

                if main:
                    @blk.tensor
                    def _(te):
                        C.wait(te, "sS")
                        for c in range(16):
                            te.matmul(p_s1[:, :], onesb[:, :], v_xnslab(c),
                                      start=(c == 0), stop=(c == 15))
                        for c in range(15):
                            te.matmul(p_s2[:, :], onesb[:, :], v_sqb(c),
                                      start=(c == 0), stop=False)
                        C.inc("sT", te.matmul(p_s2[:, :], onesb[:, :], v_sqb(15),
                                              start=False, stop=True))

                    @blk.vector
                    def _(ve, b=b):
                        C.wait(ve, "sT")
                        ve.tensor_scalar(R[0:1, 1024:1280], p_s1[:, :], 1.0 / D, None,
                                         op0=ALU.mult)
                        ve.drain()
                        ve.tensor_tensor(R[0:1, 1280:1536], R[0:1, 1024:1280],
                                         R[0:1, 1024:1280], op=ALU.mult)
                        ve.drain()
                        C.inc("sV", ve.scalar_tensor_tensor(
                            R[0:1, 1280:1536], p_s2[:, :], 1.0 / D, R[0:1, 1280:1536],
                            op0=ALU.mult, op1=ALU.subtract))

                    @blk.scalar
                    def _(sc, b=b):
                        C.wait(sc, "sV")
                        sc.activation(R2[0:1, b * 256:(b + 1) * 256],
                                      R[0:1, 1280:1536], AF.Ln, bias=epsb[0:1, :])
                        sc.drain()
                        C.inc("sS", sc.activation(
                            R2[0:1, b * 256:(b + 1) * 256],
                            R2[0:1, b * 256:(b + 1) * 256], AF.Exp, scale=-0.5))

                    @blk.vector
                    def _(ve, b=b):
                        C.wait(ve, "sS")
                        C.inc("sV", ve.scalar_tensor_tensor(
                            R2[0:1, 1024 + b * 256:1024 + (b + 1) * 256],
                            R[0:1, 1024:1280], -1.0,
                            R2[0:1, b * 256:(b + 1) * 256],
                            op0=ALU.mult, op1=ALU.mult))

                    @blk.sync
                    def _(sy, b=b):
                        C.wait(sy, "sS")
                        C.inc("dA", sy.dma_start(
                            s_xn16[b, :, :],
                            view(AR_A, BF16, 0, 128, 0, 16 * 256)))

                @blk.tensor
                def _(te, vt0=vt0):
                    C.wait(te, "sS", "dW")
                    C.wait(te, "sV")
                    for kc in range(15):
                        te.matmul(p_kv[:, :], v_xnslab(kc, 0, 128), v_wvb(kc),
                                  start=(kc == 0), stop=False)
                    C.inc("sT", te.matmul(p_kv[:, :], v_xnslab(15, 0, 128), v_wvb(15),
                                          start=False, stop=True))
                    for kc in range(15):
                        te.matmul(p_kv2[:, :], v_xnslab(kc, 128, 128), v_wvb(kc),
                                  start=(kc == 0), stop=False)
                    C.inc("sT", te.matmul(p_kv2[:, :], v_xnslab(15, 128, 128), v_wvb(15),
                                          start=False, stop=True))

                vp_box = []

                @blk.vector
                def _(ve, vt0=vt0, vp_box=vp_box):
                    ve.wait_ge(C.sem["sT"], C.cnt["sT"] - 1)
                    ve.tensor_copy(v_vs(vt0), p_kv[:, :])
                    C.wait(ve, "sT")
                    vp_box.append(C.inc("sV", ve.tensor_copy(v_vs(vt0 + 1),
                                                             p_kv2[:, :])))

                PROT = [p_rot, p_rot2]
                kv_ev = []
                for g in range(4):
                    @blk.gpsimd
                    def _(gp, g=g):
                        C.wait(gp, "sT")
                        C.inc("dW", gp.dma_start(
                            view(AR_D, BF16, 0, 128, 16 * 1024 + (g % 2) * 4096, 2048),
                            I["wk"][g].rearrange("p a b -> p (a b)"),
                            max_dma_last_dim=DMA_CAST_MAX))

                    @blk.tensor
                    def _(te, g=g, vp_box=vp_box):
                        C.wait(te, "dW", "sS")
                        if g < 2:
                            te.wait_ge(C.sem["sV"], vp_box[0])
                        else:
                            te.wait_ge(C.sem["sV"], kv_ev[g - 2])
                        pk = p_kv if g % 2 == 0 else p_kv2
                        for kc in range(15):
                            te.matmul(pk[:, 0:256], v_wkb(g % 2, kc), v_xnslab(kc),
                                      start=(kc == 0), stop=False)
                        C.inc("sT", te.matmul(pk[:, 0:256], v_wkb(g % 2, 15),
                                              v_xnslab(15), start=False, stop=True))

                    @blk.vector
                    def _(ve, g=g):
                        C.wait(ve, "sT")
                        pk = p_kv if g % 2 == 0 else p_kv2
                        kv_ev.append(C.inc("sV", ve.tensor_copy(
                            view(AR_A, BF16, 0, 128, 40 * 1024 + (g % 2) * 3072, 256),
                            pk[:, 0:256])))

                    @blk.tensor
                    def _(te, g=g):
                        te.wait_ge(C.sem["sV"], kv_ev[g])
                        C.inc("sT", te.matmul(
                            PROT[g % 2][:, :], rotb[:, :],
                            view(AR_A, BF16, 0, 128, 40 * 1024 + (g % 2) * 3072, 256),
                            start=True, stop=True))

                    @blk.vector
                    def _(ve, g=g, o0=o0, kcol0=kcol0):
                        C.wait(ve, "sT")
                        ve.tensor_tensor(v_tmp(1),
                                         view(AR_A, BF16, 0, 128,
                                              40 * 1024 + (g % 2) * 3072, 256),
                                         v_cosk(0, 128, o0, 256), op=ALU.mult)
                        ve.tensor_tensor(v_tmp(2), PROT[g % 2][:, :],
                                         v_sink(0, 128, o0, 256), op=ALU.mult)
                        ve.drain()
                        C.inc("sV", ve.tensor_tensor(
                            v_ks(g, kcol0, 256), v_tmp(1), v_tmp(2), op=ALU.add))

            @blk.sync
            def _(sy):
                C.wait(sy, "sV")
                C.wait(sy, "dA")
                C.inc("dA", sy.dma_start(s_k[:, :], view(AR_C, BF16, 0, 128, 0, 4 * 3072)))
                C.inc("dA", sy.dma_start(s_v[:, :], view(AR_C, BF16, 0, 128, 24 * 1024, 24 * 512)))
                if DBG == "p1":
                    C.inc("dO", sy.dma_start(dbg["dxn32"][:, :],
                                             view(AR_B, F32, 0, 128, 0, 16 * NT)))
                    C.inc("dO", sy.dma_start(dbg["dk"][:, :],
                                             view(AR_C, BF16, 0, 128, 0, 4 * 3072)))
                    C.inc("dO", sy.dma_start(dbg["dv"][:, :],
                                             view(AR_C, BF16, 0, 128, 24 * 1024, 24 * 512)))
                    C.inc("dO", sy.dma_start(dbg["dl1s"][:, :], R2[:, 0:2048]))
                    for bb in range(4):
                        C.inc("dO", sy.dma_start(dbg["dxn16"][bb], s_xn16[bb]))
                    sy.wait_ge(C.sem["dO"], C.cnt["dO"])

        if DBG == "p1":
            return nc

        def v_ln1(c, t0=0, nn=NT):
            return view(AR_A, BF16, 0, 128, (c * NT + t0) * 2, nn)

        def v_wslab(par, c0, nn):
            return view(AR_A, BF16, 0, 128, 32 * 1024 + par * 4096 + c0 * 2, nn)

        def v_lscr(t0=0, nn=NT):
            return view(AR_A, F32, 0, 128, 40 * 1024 + t0 * 4, nn)

        def v_qkv(cc, t0=0, nn=NT):
            return view(AR_C, BF16, 0, 128, (cc * NT + t0) * 2, nn)

        with nc.Block() as blk, \
             nc.psum_tensor("p_r1", [128, 1024], F32) as p_r1, \
             nc.psum_tensor("p_m1", [128, 1024], F32) as p_m1, \
             nc.psum_tensor("p_mma", [128, 1024], F32) as p_mma, \
             nc.psum_tensor("p_mmb", [128, 1024], F32) as p_mmb:

            @blk.tensor
            def _(te):
                C.wait(te, "sV", "sS", "dA")
                te.matmul(p_r1[:, 0:512], onesf[:, :], R2[0:1, 0:512], start=True, stop=True)
                te.matmul(p_r1[:, 512:1024], onesf[:, :], R2[0:1, 512:1024], start=True, stop=True)
                te.matmul(p_m1[:, 0:512], onesf[:, :], R2[0:1, 1024:1536], start=True, stop=True)
                C.inc("sT", te.matmul(p_m1[:, 512:1024], onesf[:, :], R2[0:1, 1536:2048],
                                      start=True, stop=True))

            @blk.vector
            def _(ve):
                C.wait(ve, "sT")
                for c in range(16):
                    ve.tensor_tensor(v_lscr(), v_xn32(c), p_r1[:, :], op=ALU.mult)
                    ve.drain()
                    ve.tensor_tensor(v_lscr(), v_lscr(), p_m1[:, :], op=ALU.add)
                    ve.drain()
                    ins = ve.tensor_scalar(v_ln1(c), v_lscr(), l1w[:, c:c + 1],
                                           l1b[:, c:c + 1], op0=ALU.mult, op1=ALU.add)
                    if c == 15:
                        C.inc("sV", ins)
                    else:
                        ve.drain()

            qkv_ev = []
            for nq in range(24):
                @blk.gpsimd
                def _(gp, nq=nq):
                    C.wait(gp, "sT")
                    C.inc("dW", gp.dma_start(
                        view(AR_A, BF16, 0, 128, 32 * 1024 + (nq % 2) * 4096, 2048),
                        I["wqkv"][nq].rearrange("p a b -> p (a b)"),
                        max_dma_last_dim=DMA_CAST_MAX))

                @blk.tensor
                def _(te, nq=nq):
                    C.wait(te, "dW")
                    if nq == 0:
                        C.wait(te, "sV", "sS")
                    elif nq >= 2:
                        te.wait_ge(C.sem["sV"], qkv_ev[nq - 2])
                    pm = p_mma if nq % 2 == 0 else p_mmb
                    for tau in range(2):
                        for kc in range(16):
                            ins = te.matmul(pm[:, tau * 512:(tau + 1) * 512],
                                            v_wslab(nq % 2, kc * 128, 128),
                                            v_ln1(kc, tau * 512, 512),
                                            start=(kc == 0), stop=(kc == 15))
                    C.inc("sT", ins)

                @blk.vector
                def _(ve, nq=nq):
                    C.wait(ve, "sT")
                    pm = p_mma if nq % 2 == 0 else p_mmb
                    ve.tensor_copy(v_qkv(nq, 0, 512), pm[:, 0:512])
                    qkv_ev.append(C.inc("sV", ve.tensor_copy(v_qkv(nq, 512, 512),
                                                             pm[:, 512:1024])))

        if DBG == "p2a":
            with nc.Block() as blk:
                @blk.sync
                def _(sy):
                    C.wait(sy, "sS", "sV")
                    C.inc("dO", sy.dma_start(dbg["dln1"][:, :],
                                             view(AR_A, BF16, 0, 128, 0, 16 * NT)))
                    C.inc("dO", sy.dma_start(dbg["dqkv"][:, :],
                                             view(AR_C, BF16, 0, 128, 0, 24 * NT)))
                    sy.wait_ge(C.sem["dO"], C.cnt["dO"])
            return nc

        def v_qtm(h, c0=0, nn=128):
            return view(AR_A, BF16, 0, 128, (h * 128 + c0) * 2, nn)

        def v_krep(bt, h, nn=128):
            return view(AR_A, BF16, 0, 128, 4 * 1024 + (bt * 2048 + h * 128) * 2, nn)

        def v_vtm(bt, g, nn=128):
            return view(AR_A, BF16, 0, 128, 20 * 1024 + (bt * 512 + g * 128) * 2, nn)

        def v_pf(c0=0, nn=2048):
            return view(AR_A, BF16, 0, 128, 24 * 1024 + c0 * 2, nn)

        def v_acc(c0=0, nn=2048):
            return view(AR_A, F32, 0, 128, 28 * 1024 + c0 * 4, nn)

        def v_accb(c0=0, nn=2048):
            return view(AR_A, BF16, 0, 128, 36 * 1024 + c0 * 2, nn)

        def v_sp(i, nn=16):
            return view(AR_A, F32, 0, 128, 40 * 1024 + i * 64, nn)

        def v_ao2(h, t0=0, nn=NT):
            return view(AR_D, BF16, 0, 128, (h * NT + t0) * 2, nn)

        S128 = 1.0 / math.sqrt(128.0)

        with nc.Block() as blk, \
             nc.psum_tensor("ptr0", [128, 128], BF16) as ptr0, \
             nc.psum_tensor("ptr1", [128, 128], BF16) as ptr1:
            PTR = [ptr0, ptr1]
            for ht in range(2):
                for bt in range(4):
                    t0 = bt * 256 + ht * 128
                    for g in range(4):
                        @blk.tensor
                        def _(te, bt=bt, g=g, t0=t0):
                            C.wait(te, "sS", "sV")
                            te.transpose(PTR[0][:, :], v_qkv(6 * g + 4, t0, 128),
                                         identb[:, :])
                            C.inc("sT", te.transpose(PTR[1][:, :],
                                                     v_qkv(6 * g + 5, t0, 128),
                                                     identb[:, :]))

                        @blk.scalar
                        def _(sc, bt=bt, g=g):
                            C.wait(sc, "sT")
                            sc.copy(v_krep(bt, 4 * g), PTR[0][:, :])
                            C.inc("sS", sc.copy(v_krep(bt, 4 * g + 1), PTR[0][:, :]))

                        @blk.vector
                        def _(ve, bt=bt, g=g):
                            C.wait(ve, "sT")
                            ve.tensor_copy(v_krep(bt, 4 * g + 2), PTR[0][:, :])
                            ve.tensor_copy(v_krep(bt, 4 * g + 3), PTR[0][:, :])
                            C.inc("sV", ve.tensor_copy(v_vtm(bt, g), PTR[1][:, :]))

                for bs in range(4):
                    t0 = bs * 256 + ht * 128
                    for h in range(16):
                        @blk.tensor
                        def _(te, h=h, t0=t0):
                            if h == 0:
                                C.wait(te, "sS", "sV")
                            else:
                                C.wait(te, "sS" if h % 2 == 0 else "sV")
                            cc = 6 * (h // 4) + (h % 4)
                            C.inc("sT", te.transpose(PTR[h % 2][:, :],
                                                     v_qkv(cc, t0, 128), identb[:, :]))

                        if h % 2 == 0:
                            @blk.scalar
                            def _(sc, h=h):
                                C.wait(sc, "sT")
                                C.inc("sS", sc.copy(v_qtm(h), PTR[0][:, :]))
                        else:
                            @blk.vector
                            def _(ve, h=h):
                                C.wait(ve, "sT")
                                C.inc("sV", ve.tensor_copy(v_qtm(h), PTR[1][:, :]))

                    @blk.vector
                    def _(ve, bs=bs):
                        C.wait(ve, "sS")
                        ve.drain()
                        for bt in range(bs + 1):
                            ve.tensor_tensor(v_pf(), v_qtm(0, 0, 2048),
                                             v_krep(bt, 0, 2048), op=ALU.mult)
                            ve.drain()
                            ve.reduce_sum(
                                v_sp(bt),
                                v_pf().rearrange("p (a b) -> p a b", b=128), axis=AX.X)
                            ve.drain()
                        ve.tensor_copy(v_sp(8), v_sp(0))
                        for bt in range(1, bs + 1):
                            ve.drain()
                            ve.tensor_tensor(v_sp(8), v_sp(8), v_sp(bt), op=ALU.max)
                        ve.drain()
                        ve.tensor_scalar(v_sp(9), v_sp(8), -S128, None, op0=ALU.mult)
                        ve.drain()
                        for bt in range(bs + 1):
                            ins = ve.scalar_tensor_tensor(
                                v_sp(4 + bt), v_sp(bt), S128, v_sp(9),
                                op0=ALU.mult, op1=ALU.add)
                        C.inc("sV", ins)

                    @blk.scalar
                    def _(sc, bs=bs):
                        C.wait(sc, "sV")
                        for bt in range(bs + 1):
                            ins = sc.activation(v_sp(4 + bt), v_sp(4 + bt), AF.Exp)
                        C.inc("sS", ins)

                    @blk.vector
                    def _(ve, bs=bs, t0=t0):
                        C.wait(ve, "sS")
                        ve.tensor_copy(v_sp(10), v_sp(4))
                        for bt in range(1, bs + 1):
                            ve.drain()
                            ve.tensor_tensor(v_sp(10), v_sp(10), v_sp(4 + bt), op=ALU.add)
                        ve.drain()
                        ve.reciprocal(v_sp(11), v_sp(10))
                        ve.drain()
                        for bt in range(bs + 1):
                            ve.tensor_tensor(v_sp(4 + bt), v_sp(4 + bt), v_sp(11),
                                             op=ALU.mult)
                        ve.drain()
                        for bt in range(bs + 1):
                            for h in range(16):
                                g = h // 4
                                if bt == 0:
                                    ins = ve.tensor_scalar(
                                        v_acc(h * 128, 128), v_vtm(bt, g),
                                        v_sp(4 + bt)[:, h:h + 1], None, op0=ALU.mult)
                                else:
                                    ins = ve.scalar_tensor_tensor(
                                        v_acc(h * 128, 128), v_vtm(bt, g),
                                        v_sp(4 + bt)[:, h:h + 1], v_acc(h * 128, 128),
                                        op0=ALU.mult, op1=ALU.add)
                            if bt < bs:
                                ve.drain()
                        C.inc("sV", ins)

                    @blk.scalar
                    def _(sc):
                        C.wait(sc, "sV")
                        C.inc("sS", sc.copy(v_accb(), v_acc()))

                    for h in range(16):
                        @blk.tensor
                        def _(te, h=h):
                            if h <= 1:
                                C.wait(te, "sS", "sV")
                            else:
                                C.wait(te, "sS" if h % 2 == 0 else "sV")
                            C.inc("sT", te.transpose(PTR[h % 2][:, :],
                                                     v_accb(h * 128, 128), identb[:, :]))

                        if h % 2 == 0:
                            @blk.scalar
                            def _(sc, h=h, t0=t0):
                                C.wait(sc, "sT")
                                C.inc("sS", sc.copy(v_ao2(h, t0, 128), PTR[0][:, :]))
                        else:
                            @blk.vector
                            def _(ve, h=h, t0=t0):
                                C.wait(ve, "sT")
                                C.inc("sV", ve.tensor_copy(v_ao2(h, t0, 128),
                                                           PTR[1][:, :]))

        if DBG == "p2b":
            with nc.Block() as blk:
                @blk.sync
                def _(sy):
                    C.wait(sy, "sS", "sV")
                    C.inc("dO", sy.dma_start(dbg["dao2"][:, :],
                                             view(AR_D, BF16, 0, 128, 0, 16 * NT)))
                    sy.wait_ge(C.sem["dO"], C.cnt["dO"])
            return nc

        def v_xmoe(c, t0=0, nn=NT):
            return view(AR_C, BF16, 0, 128, (c * NT + t0) * 2, nn)

        gfm = view(AR_C, F32, 0, 8, 32 * 1024, NT)

        with nc.Block() as blk, \
             nc.psum_tensor("p_mc", [128, 1024], F32) as p_mc, \
             nc.psum_tensor("p_md", [128, 1024], F32) as p_md:
            prj_ev = []
            for nq in range(16):
                @blk.gpsimd
                def _(gp, nq=nq):
                    C.wait(gp, "sT")
                    C.inc("dW", gp.dma_start(
                        view(AR_A, BF16, 0, 128, 32 * 1024 + (nq % 2) * 4096, 2048),
                        I["wproj"][nq].rearrange("p a b -> p (a b)"),
                        max_dma_last_dim=DMA_CAST_MAX))

                @blk.tensor
                def _(te, nq=nq):
                    C.wait(te, "dW")
                    if nq == 0:
                        C.wait(te, "sS", "sV")
                    elif nq >= 2:
                        te.wait_ge(C.sem["sV"], prj_ev[nq - 2])
                    pm = p_mc if nq % 2 == 0 else p_md
                    for tau in range(2):
                        for kc in range(16):
                            ins = te.matmul(pm[:, tau * 512:(tau + 1) * 512],
                                            v_wslab(nq % 2, kc * 128, 128),
                                            v_ao2(kc, tau * 512, 512),
                                            start=(kc == 0), stop=(kc == 15))
                    C.inc("sT", ins)

                @blk.vector
                def _(ve, nq=nq):
                    C.wait(ve, "sT")
                    pm = p_mc if nq % 2 == 0 else p_md
                    ve.tensor_tensor(v_xn32(nq, 0, 512), pm[:, 0:512],
                                     v_xn32(nq, 0, 512), op=ALU.add)
                    prj_ev.append(C.inc("sV", ve.tensor_tensor(
                        v_xn32(nq, 512, 512), pm[:, 512:1024],
                        v_xn32(nq, 512, 512), op=ALU.add)))

        def v_x2(par, nn=NT):
            return view(AR_A, BF16, 0, 128, 32 * 1024 + par * 2048, nn)

        def v_s2b(par, nn=NT):
            return view(AR_A, BF16, 0, 128, 36 * 1024 + par * 2048, nn)

        with nc.Block() as blk, \
             nc.psum_tensor("p_l1", [1, 1024], F32) as p_l1, \
             nc.psum_tensor("p_l2", [1, 1024], F32) as p_l2:
            for c in range(16):
                @blk.vector
                def _(ve, c=c):
                    C.wait(ve, "sT")
                    C.inc("sV", ve.tensor_copy(v_x2(c % 2), v_xn32(c)))

                @blk.scalar
                def _(sc, c=c):
                    C.wait(sc, "sT")
                    C.inc("sS", sc.activation(v_s2b(c % 2), v_xn32(c), AF.Square))

                @blk.tensor
                def _(te, c=c):
                    C.wait(te, "sV", "sS")
                    for half in range(2):
                        te.matmul(p_l1[:, half * 512:(half + 1) * 512], onesb[:, :],
                                  view(AR_A, BF16, 0, 128,
                                       32 * 1024 + (c % 2) * 2048 + half * 1024, 512),
                                  start=(c == 0), stop=(c == 15))
                        ins = te.matmul(p_l2[:, half * 512:(half + 1) * 512], onesb[:, :],
                                        view(AR_A, BF16, 0, 128,
                                             36 * 1024 + (c % 2) * 2048 + half * 1024, 512),
                                        start=(c == 0), stop=(c == 15))
                    C.inc("sT", ins)

            for qt in range(4):
                cs = slice(qt * 256, (qt + 1) * 256)

                @blk.vector
                def _(ve, qt=qt, cs=cs):
                    C.wait(ve, "sT", "sS")
                    ve.drain()
                    ve.tensor_scalar(R[0:1, 1024:1280], p_l1[:, cs], 1.0 / D, None,
                                     op0=ALU.mult)
                    ve.drain()
                    ve.tensor_tensor(R[0:1, 1280:1536], R[0:1, 1024:1280],
                                     R[0:1, 1024:1280], op=ALU.mult)
                    ve.drain()
                    C.inc("sV", ve.scalar_tensor_tensor(
                        R[0:1, 1280:1536], p_l2[:, cs], 1.0 / D, R[0:1, 1280:1536],
                        op0=ALU.mult, op1=ALU.subtract))

                @blk.scalar
                def _(sc, qt=qt, cs=cs):
                    C.wait(sc, "sV")
                    sc.activation(R2[0:1, cs], R[0:1, 1280:1536], AF.Ln,
                                  bias=epsb[0:1, :])
                    sc.drain()
                    C.inc("sS", sc.activation(R2[0:1, cs], R2[0:1, cs],
                                              AF.Exp, scale=-0.5))

                @blk.vector
                def _(ve, qt=qt, cs=cs):
                    C.wait(ve, "sS")
                    C.inc("sV", ve.scalar_tensor_tensor(
                        R2[0:1, 1024 + qt * 256:1024 + (qt + 1) * 256],
                        R[0:1, 1024:1280], -1.0, R2[0:1, cs],
                        op0=ALU.mult, op1=ALU.mult))

        with nc.Block() as blk, \
             nc.psum_tensor("p_r2", [128, 1024], F32) as p_r2, \
             nc.psum_tensor("p_m2", [128, 1024], F32) as p_m2:
            @blk.tensor
            def _(te):
                C.wait(te, "sV", "sS")
                te.matmul(p_r2[:, 0:512], onesf[:, :], R2[0:1, 0:512], start=True, stop=True)
                te.matmul(p_r2[:, 512:1024], onesf[:, :], R2[0:1, 512:1024], start=True, stop=True)
                te.matmul(p_m2[:, 0:512], onesf[:, :], R2[0:1, 1024:1536], start=True, stop=True)
                C.inc("sT", te.matmul(p_m2[:, 512:1024], onesf[:, :], R2[0:1, 1536:2048],
                                      start=True, stop=True))

            @blk.vector
            def _(ve):
                C.wait(ve, "sT")
                for c in range(16):
                    ve.tensor_tensor(v_lscr(), v_xn32(c), p_r2[:, :], op=ALU.mult)
                    ve.drain()
                    ve.tensor_tensor(v_lscr(), v_lscr(), p_m2[:, :], op=ALU.add)
                    ve.drain()
                    ins = ve.tensor_scalar(v_xmoe(c), v_lscr(), l2w[:, c:c + 1],
                                           l2b[:, c:c + 1], op0=ALU.mult, op1=ALU.add)
                    if c == 15:
                        C.inc("sV", ins)
                    else:
                        ve.drain()

        def v_g8(i, nn=8):
            return view(AR_A, F32, 0, 128, 40 * 1024 + i * 32, nn)

        with nc.Block() as blk, \
             nc.psum_tensor("p_rt", [128, 8], F32) as p_rt, \
             nc.psum_tensor("p_gt", [8, 128], F32) as p_gt:
            for tc in range(8):
                @blk.tensor
                def _(te, tc=tc):
                    C.wait(te, "sV", "dW")
                    for kc in range(15):
                        te.matmul(p_rt[:, :], v_xmoe(kc, tc * 128, 128),
                                  wroutb[:, kc * 8:kc * 8 + 8], start=(kc == 0), stop=False)
                    C.inc("sT", te.matmul(p_rt[:, :], v_xmoe(15, tc * 128, 128),
                                          wroutb[:, 120:128], start=False, stop=True))

                @blk.vector
                def _(ve, tc=tc):
                    C.wait(ve, "sT")
                    ve.reduce_max(v_g8(20, 1), p_rt[:, :], axis=AX.X, negate=True)
                    C.inc("sV", ve.tensor_copy(v_g8(0), p_rt[:, :]))

                @blk.scalar
                def _(sc, tc=tc):
                    C.wait(sc, "sV")
                    C.inc("sS", sc.activation(v_g8(4), v_g8(0), AF.Exp,
                                              bias=v_g8(20, 1)))

                @blk.vector
                def _(ve, tc=tc):
                    C.wait(ve, "sS")
                    ve.reduce_max(v_g8(21, 1), v_g8(4), axis=AX.X)
                    ve.drain()
                    ve.tensor_scalar(v_g8(8), v_g8(4), v_g8(21, 1), None, op0=ALU.is_equal)
                    ve.drain()
                    ve.scalar_tensor_tensor(v_g8(12), v_g8(8), NEG, v_g8(4),
                                            op0=ALU.mult, op1=ALU.add)
                    ve.drain()
                    ve.reduce_max(v_g8(22, 1), v_g8(12), axis=AX.X)
                    ve.drain()
                    ve.tensor_tensor(v_g8(23, 1), v_g8(21, 1), v_g8(22, 1), op=ALU.add)
                    ve.drain()
                    ve.reciprocal(v_g8(24, 1), v_g8(23, 1))
                    ve.drain()
                    ve.tensor_scalar(v_g8(8), v_g8(4), v_g8(22, 1), None, op0=ALU.is_ge)
                    ve.drain()
                    ve.tensor_tensor(v_g8(8), v_g8(4), v_g8(8), op=ALU.mult)
                    ve.drain()
                    C.inc("sV", ve.tensor_scalar(v_g8(8), v_g8(8), v_g8(24, 1), None,
                                                 op0=ALU.mult))

                @blk.tensor
                def _(te, tc=tc):
                    C.wait(te, "sV")
                    C.inc("sT", te.transpose(p_gt[:, :], v_g8(8), identf[:, :]))

                @blk.scalar
                def _(sc, tc=tc):
                    C.wait(sc, "sT")
                    C.inc("sS", sc.copy(gfm[:, tc * 128:(tc + 1) * 128], p_gt[:, :]))

        if DBG == "p2c":
            with nc.Block() as blk:
                @blk.sync
                def _(sy):
                    C.wait(sy, "sS", "sV")
                    C.inc("dO", sy.dma_start(dbg["dh2"][:, :],
                                             view(AR_B, F32, 0, 128, 0, 16 * NT)))
                    C.inc("dO", sy.dma_start(dbg["dxmoe"][:, :],
                                             view(AR_C, BF16, 0, 128, 0, 16 * NT)))
                    C.inc("dO", sy.dma_start(dbg["dgfm"][:, :], gfm))
                    sy.wait_ge(C.sem["dO"], C.cnt["dO"])
            return nc

        def v_htil(n, c0=0, nn=512):
            return view(AR_A, BF16, 0, 128, (n * 512 + c0) * 2, nn)

        def v_w1b(par, kc):
            return view(AR_A, BF16, 0, 128,
                        32 * 1024 + par * 4096 + kc * 256, 128)

        def v_w2b(par, ncb):
            return view(AR_D, BF16, 0, 128, par * 8192 + ncb * 256, 128)

        def v_gt3(par):
            return view(AR_D, BF16, 0, 128, 16 * 1024 + par * 1024, 512)

        gbv = view(AR_D, F32, 0, 128, 18 * 1024, 512)
        oneh8s = view(AR_D, F32, 0, 8, 20 * 1024, 1024)

        with nc.Block() as blk, \
             nc.psum_tensor("p_ha", [128, 512], F32) as p_ha, \
             nc.psum_tensor("p_hb", [128, 512], F32) as p_hb, \
             nc.psum_tensor("p_hc", [128, 512], F32) as p_hc, \
             nc.psum_tensor("p_hd", [128, 512], F32) as p_hd, \
             nc.psum_tensor("p_ya", [128, 512], F32) as p_ya, \
             nc.psum_tensor("p_yb", [128, 512], F32) as p_yb, \
             nc.psum_tensor("p_yc", [128, 512], F32) as p_yc, \
             nc.psum_tensor("p_yd", [128, 512], F32) as p_yd:
            PH = [[p_ha, p_hb], [p_hc, p_hd]]
            PY = [[p_ya, p_yb], [p_yc, p_yd]]

            def v_gt3(tau, par):
                return view(AR_D, BF16, 0, 128, 16 * 1024 + (tau * 2 + par) * 1024, 512)

            def v_gbv(tau):
                return view(AR_D, F32, 0, 128, 20 * 1024 + tau * 2048, 512)

            oneh8s2 = view(AR_D, F32, 0, 8, 24 * 1024, 1024)

            def v_htil2(tau, n, c0=0, nn=512):
                return view(AR_A, BF16, 0, 128, (tau * 16 + n) * 1024 + c0 * 2, nn)

            @blk.sync
            def _(sy):
                C.inc("dA", sy.dma_start(oneh8s2, I["oneh8"][:, :]))

            for e in range(NE):
                for tau in range(2):
                    @blk.tensor
                    def _(te, e=e, tau=tau):
                        C.wait(te, "dA", "sS", "sV")
                        C.inc("sT", te.matmul(p_ya[:, :],
                                              oneh8s2[:, e * 128:(e + 1) * 128],
                                              gfm[:, tau * 512:(tau + 1) * 512],
                                              start=True, stop=True))

                    @blk.vector
                    def _(ve, tau=tau):
                        C.wait(ve, "sT")
                        C.inc("sV", ve.tensor_copy(v_gbv(tau), p_ya[:, :]))

                for half in range(2):
                    n0 = half * 16
                    T_h = C.cnt["sT"]
                    S_h = C.cnt["sS"]
                    V_h = C.cnt["sV"]
                    WB = [C.cnt["dW0"], C.cnt["dW1"], C.cnt["dW2"]]

                    @blk.gpsimd
                    def _(gp, e=e, half=half, n0=n0, T_h=T_h):
                        for n in range(16):
                            if n >= 3:
                                wv = T_h + n - 2
                            elif e == 0 and half == 0:
                                wv = T_h
                            else:
                                wv = T_h - 3 + n
                            gp.wait_ge(C.sem["sT"], wv)
                            C.inc("dW%d" % (n % 3), gp.dma_start(
                                view(AR_A, BF16, 0, 128, 32 * 1024 + (n % 3) * 4096,
                                     2048),
                                I["w1"][e, n0 + n].rearrange("p a b -> p (a b)"),
                                max_dma_last_dim=DMA_CAST_MAX))

                    @blk.scalar
                    def _(sc, T_h=T_h, V_h=V_h):
                        for n in range(16):
                            sc.wait_ge(C.sem["sT"], T_h + n + 1)
                            if n >= 2:
                                sc.wait_ge(C.sem["sV"], V_h + n - 1)
                            sc.activation(v_gt3(0, n % 2), PH[0][n % 2][:, :], AF.Gelu)
                            C.inc("sS", sc.activation(v_gt3(1, n % 2),
                                                      PH[1][n % 2][:, :], AF.Gelu))

                    for n in range(16):
                        @blk.tensor
                        def _(te, n=n, WB=WB, S_h=S_h):
                            te.wait_ge(C.sem["dW%d" % (n % 3)],
                                       WB[n % 3] + 16 * (n // 3 + 1))
                            if n >= 2:
                                te.wait_ge(C.sem["sS"], S_h + n - 1)
                            for kc in range(16):
                                for tau in range(2):
                                    ins = te.matmul(PH[tau][n % 2][:, :],
                                                    v_w1b(n % 3, kc),
                                                    v_xmoe(kc, tau * 512, 512),
                                                    start=(kc == 0), stop=(kc == 15))
                            C.inc("sT", ins)

                        @blk.vector
                        def _(ve, n=n, S_h=S_h):
                            ve.wait_ge(C.sem["sS"], S_h + n + 1)
                            if n == 0:
                                ve.drain()
                            ve.tensor_tensor(v_htil2(0, n), v_gt3(0, n % 2),
                                             v_gbv(0), op=ALU.mult)
                            C.inc("sV", ve.tensor_tensor(v_htil2(1, n),
                                                         v_gt3(1, n % 2),
                                                         v_gbv(1), op=ALU.mult))

                    T2 = C.cnt["sT"]
                    V2 = C.cnt["sV"]
                    WB2 = [C.cnt["dW3"], C.cnt["dW4"]]

                    @blk.gpsimd
                    def _(gp, e=e, half=half, T2=T2):
                        for dc in range(16):
                            gp.wait_ge(C.sem["sT"],
                                       T2 + dc - 1 if dc >= 2 else T2 - 3 + dc)
                            C.inc("dW%d" % (dc % 2 + 3), gp.dma_start(
                                view(AR_D, BF16, 0, 128, (dc % 2) * 4096, 2048),
                                I["w2"][e, half, dc].rearrange("p a b -> p (a b)"),
                                max_dma_last_dim=DMA_CAST_MAX))

                    for dc in range(16):
                        @blk.tensor
                        def _(te, dc=dc, WB2=WB2, V2=V2):
                            te.wait_ge(C.sem["dW%d" % (dc % 2 + 3)],
                                       WB2[dc % 2] + 16 * (dc // 2 + 1))
                            te.wait_ge(C.sem["sV"],
                                       V2 + dc - 1 if dc >= 2 else V2)
                            for nc_ in range(16):
                                for tau in range(2):
                                    ins = te.matmul(
                                        PY[tau][dc % 2][:, :],
                                        view(AR_D, BF16, 0, 128,
                                             (dc % 2) * 4096 + nc_ * 256, 128),
                                        v_htil2(tau, nc_),
                                        start=(nc_ == 0), stop=(nc_ == 15))
                            C.inc("sT", ins)

                        @blk.vector
                        def _(ve, dc=dc, T2=T2):
                            ve.wait_ge(C.sem["sT"], T2 + dc + 1)
                            ve.tensor_tensor(
                                v_xn32(dc, 0, 512), PY[0][dc % 2][:, :],
                                v_xn32(dc, 0, 512), op=ALU.add)
                            C.inc("sV", ve.tensor_tensor(
                                v_xn32(dc, 512, 512), PY[1][dc % 2][:, :],
                                v_xn32(dc, 512, 512), op=ALU.add))

            @blk.scalar
            def _(sc):
                C.wait(sc, "sV")
                for c in range(15):
                    sc.activation(v_xn32(c), v_xn32(c), AF.Silu)
                C.inc("sS", sc.activation(v_xn32(15), v_xn32(15), AF.Silu))

            @blk.sync
            def _(sy):
                C.wait(sy, "sS")
                for c in range(16):
                    C.inc("dA", sy.dma_start(s_g[c], v_xn32(c)))

        if DBG == "p3":
            with nc.Block() as blk:
                @blk.sync
                def _(sy):
                    C.wait(sy, "dA")
                    for c in range(16):
                        C.inc("dO", sy.dma_start(dbg["dg"][c], s_g[c]))
                    sy.wait_ge(C.sem["dO"], C.cnt["dO"])
            return nc

        def v_xn16r(c, t0=0, nn=NT):
            return view(AR_A, BF16, 0, 128, (c * NT + t0) * 2, nn)

        def v_cosq(p0, np_, c0, nn):
            return view(AR_A, F32, p0, np_, 32 * 1024 + c0 * 4, nn)

        def v_sinq(p0, np_, c0, nn):
            return view(AR_A, F32, p0, np_, 36 * 1024 + c0 * 4, nn)

        def v_mask(qt, nn=640):
            return view(AR_D, BF16, 0, 128, 18 * 1024 + qt * 1280, nn)

        def v_qrb(p0, np_, c0, nn):
            return view(AR_A, BF16, p0, np_, 40 * 1024 + c0 * 2, nn)

        def v_dfm(c0=0, nn=640):
            return view(AR_D, BF16, 0, 128, 21 * 1024 + c0 * 2, nn)

        def v_ao(h, t0=0, nn=NT):
            return view(AR_B, BF16, 0, 128, (h * NT + t0) * 2, nn)

        SCR = 32 * 1024
        def v_sb(i, p0=0, np_=128, c0=0, nn=640):
            return view(AR_B, F32, p0, np_, SCR + i * 2560 + c0 * 4, nn)

        def v_wq4(par, c0, nn):
            return view(AR_B, BF16, 0, 128, 48 * 1024 + par * 4096 + c0 * 2, nn)

        def v_gt4(par, nn=NT):
            return view(AR_B, F32, 0, 128, 56 * 1024 + par * 4096, nn)

        def v_xt4(par, nn=NT):
            return view(AR_D, F32, 0, 128, par * 4096, nn)

        def v_ob(par, nn=NT):
            return view(AR_D, F32, 0, 128, 8 * 1024 + par * 4096, nn)

        def v_row(i, par=0):
            return view(AR_D, F32, 0, 128, 16 * 1024 + par * 32 + i * 4, 1)

        def v_sbp(idx, par, nn=640):
            if par == 0:
                return v_sb(idx if idx != 2 else 5, 0, 128, 0, nn)
            return view(AR_D, F32, 0, 128, 23040 + idx * 2560, nn)

        def v_sbpb(idx, par, nn=640):
            if par == 0:
                return view(AR_B, BF16, 0, 128, SCR + idx * 2560, nn)
            return view(AR_D, BF16, 0, 128, 23040 + idx * 2560, nn)

        def v_eb(idx, par, nn=640):
            return view(AR_B, BF16, 0, 128, SCR + (2 + idx) * 2560 + par * 1280, nn)

        with nc.Block() as blk:
            @blk.sync
            def _(sy):
                C.wait(sy, "dA")
                for b in range(4):
                    C.inc("dA", sy.dma_start(
                        view(AR_A, BF16, 0, 128, b * 512, 0 + 4096)
                        .rearrange("p (c u) -> p c u", u=1024)[:, :, 0:256] if False
                        else view(AR_A, BF16, 0, 128, 0, 16 * NT)
                        .rearrange("p (c u) -> p c u", u=NT)[:, :, b * 256:(b + 1) * 256],
                        s_xn16[b].rearrange("p (c u) -> p c u", u=256)))
                C.inc("dA", sy.dma_start(view(AR_C, BF16, 0, 128, 0, 4 * 3072), s_k[:, :]))
                C.inc("dA", sy.dma_start(view(AR_C, BF16, 0, 128, 24 * 1024, 24 * 512),
                                         s_v[:, :]))
                C.inc("dA", sy.dma_start(v_cosq(0, 128, 0, NT), I["cqh"][:, :]))
                C.inc("dA", sy.dma_start(v_sinq(0, 128, 0, NT), I["sqh"][:, :]))
                C.inc("dA", sy.dma_start(v_sb(0), I["cql"][:, 0:640]))
                C.inc("dA", sy.dma_start(v_sb(0, 0, 128, 640, 384), I["cql"][:, 640:1024]))
                C.inc("dA", sy.dma_start(v_sb(2), I["sql"][:, 0:640]))
                C.inc("dA", sy.dma_start(v_sb(2, 0, 128, 640, 384), I["sql"][:, 640:1024]))


            @blk.gpsimd
            def _(gp):
                C.inc("dW", gp.dma_start(v_mask(0), I["mask"][0],
                                         max_dma_last_dim=DMA_CAST_MAX))
                C.inc("dW", gp.dma_start(v_mask(1), I["mask"][1],
                                         max_dma_last_dim=DMA_CAST_MAX))

            @blk.vector
            def _(ve):
                C.wait(ve, "dA", "dW")
                ve.tensor_tensor(v_cosq(0, 128, 0, NT), v_cosq(0, 128, 0, NT),
                                 view(AR_B, F32, 0, 128, SCR, NT), op=ALU.add)
                C.inc("sV", ve.tensor_tensor(v_sinq(0, 128, 0, NT),
                                             v_sinq(0, 128, 0, NT),
                                             view(AR_B, F32, 0, 128, SCR + 2 * 2560, NT),
                                             op=ALU.add))

        with nc.Block() as blk, \
             nc.psum_tensor("p_s1h", [128, 640], F32) as p_s1h, \
             nc.psum_tensor("p_s2h", [128, 640], F32) as p_s2h, \
             nc.psum_tensor("p_t4a", [128, 128], BF16) as p_t4a, \
             nc.psum_tensor("p_t4b", [128, 128], BF16) as p_t4b, \
             nc.psum_tensor("p_av", [128, 128], F32) as p_av:
            PT4 = [p_t4a, p_t4b]
            sT_ = C.sem["sT"]; sV_ = C.sem["sV"]; sS_ = C.sem["sS"]
            prevm = None
            for h in range(16):
                g = h // 4

                @blk.gpsimd
                def _(gp, h=h):
                    C.wait(gp, "sT")
                    C.inc("dW", gp.dma_start(
                        view(AR_B, BF16, 0, 128, 48 * 1024 + (h % 2) * 4096, 2048),
                        I["wq"][h].rearrange("p a b -> p (a b)"),
                        max_dma_last_dim=DMA_CAST_MAX))
                    C.inc("dW", gp.dma_start(
                        view(AR_B, BF16, 0, 128, 56 * 1024 + (h % 2) * 4096, 2048),
                        I["wqp"][h].rearrange("p a b -> p (a b)"),
                        max_dma_last_dim=DMA_CAST_MAX))

                rope_v = {}
                for tau in range(2):
                    @blk.tensor
                    def _(te, h=h, tau=tau, prevm=prevm, rope_v=rope_v):
                        C.wait(te, "dW")
                        if prevm is None:
                            C.wait(te, "sV", "sS")
                        elif tau == 0:
                            te.wait_ge(sV_, prevm["b17"])
                        else:
                            te.wait_ge(sV_, rope_v[0])
                        for kc in range(16):
                            te.matmul(p_s1h[:, 0:512], v_wq4(h % 2, kc * 128, 128),
                                      v_xn16r(kc, tau * 512, 512),
                                      start=(kc == 0), stop=(kc == 15))
                        for kc in range(15):
                            te.matmul(p_s2h[:, 0:512],
                                      view(AR_B, BF16, 0, 128,
                                           56 * 1024 + (h % 2) * 4096 + kc * 256, 128),
                                      v_xn16r(kc, tau * 512, 512),
                                      start=(kc == 0), stop=False)
                        C.inc("sT", te.matmul(
                            p_s2h[:, 0:512],
                            view(AR_B, BF16, 0, 128,
                                 56 * 1024 + (h % 2) * 4096 + 15 * 256, 128),
                            v_xn16r(15, tau * 512, 512), start=False, stop=True))

                    @blk.vector
                    def _(ve, tau=tau, prevm=prevm):
                        ve.wait_ge(sT_, C.cnt["sT"])
                        if prevm is not None and tau == 0:
                            ve.wait_ge(C.sem["sS"], prevm["e6"])
                        cs = tau * 512
                        ve.tensor_tensor(v_sb(0, 0, 128, 0, 512), p_s1h[:, 0:512],
                                         v_cosq(0, 128, cs, 512), op=ALU.mult)
                        ve.tensor_tensor(v_sb(1, 0, 128, 0, 512), p_s2h[:, 0:512],
                                         v_sinq(0, 128, cs, 512), op=ALU.mult)
                        ve.drain()
                        C.inc("sV", ve.tensor_tensor(
                            v_qrb(0, 128, cs, 512), v_sb(0, 0, 128, 0, 512),
                            v_sb(1, 0, 128, 0, 512), op=ALU.add))
                    rope_v[tau] = C.cnt["sV"]

                T0 = C.cnt["sT"]; V0 = C.cnt["sV"]; S0 = C.cnt["sS"]
                order_T, order_V, order_S = [], [], []
                order_T += [("A", 0), ("A", 1), ("A", 2)]
                for i in range(5):
                    order_T += [("G", i), ("A", i + 3)]
                order_T += [("G", 5), ("G", 6), ("G", 7)]
                order_V += [("B", 0), ("B", 1)]
                for i in range(6):
                    order_V += [("D", i), ("F", i), ("B", i + 2)]
                order_V += [("D", 6), ("F", 6), ("D", 7), ("F", 7)]
                order_S += [("C", 0), ("E", 0), ("C", 1)]
                for i in range(2, 8):
                    order_S += [("H", i - 2), ("E", i - 1), ("C", i)]
                order_S += [("H", 6), ("E", 7), ("H", 7)]
                tT, tV, tS = {}, {}, {}
                cnt = T0
                for it in order_T:
                    cnt += 1 if it[0] == "A" else 6
                    tT[it] = cnt
                cnt = V0
                for it in order_V:
                    cnt += 1
                    tV[it] = cnt
                cnt = S0
                for it in order_S:
                    cnt += 1 if it[0] in ("C", "E") else 6
                    tS[it] = cnt

                def bump(name, instr, val):
                    instr.then_inc(C.sem[name], 1)
                    assert val == C.cnt[name] + 1, (name, val, C.cnt[name])
                    C.cnt[name] = val

                @blk.scalar
                def _(sc, h=h):
                    for it in order_S:
                        kind, i = it
                        par = i % 2
                        if kind == "C":
                            sc.wait_ge(sV_, tV[("B", i)])
                            if i >= 2:
                                sc.wait_ge(sV_, tV[("F", i - 2)])
                            sc.activation(v_eb(0, par), v_sbp(0, par), AF.Exp,
                                          bias=v_row(0, par), accum_out=v_row(2, par))
                            bump("sS", sc.activation(v_eb(1, par), v_sbp(1, par),
                                                     AF.Exp, bias=v_row(1, par),
                                                     accum_out=v_row(3, par)), tS[it])
                        elif kind == "E":
                            sc.wait_ge(sV_, tV[("D", i)])
                            bump("sS", sc.activation(
                                v_sbp(2, par),
                                view(AR_B, BF16, 0, 128, SCR + 4 * 2560, 640),
                                AF.Relu, scale=-1.0, accum_out=v_row(6)), tS[it])
                        else:
                            base = tS[it] - 6
                            for j in range(5):
                                sc.wait_ge(sT_, tT[("G", i)] - 6 + 1 + j)
                                base += 1
                                bump("sS", sc.copy(
                                    view(AR_D, BF16, 0, 128, 30720 + j * 256, 128),
                                    PT4[j % 2][:, :]), base)
                            sc.wait_ge(sT_, tT[("G", i)])
                            qc = (i // 2) * 256 + (i % 2) * 128
                            bump("sS", sc.copy(v_ao(h, qc, 128), p_av[:, :]), tS[it])

                @blk.vector
                def _(ve, h=h):
                    for it in order_V:
                        kind, i = it
                        par = i % 2
                        if kind == "B":
                            qt = i % 2
                            ve.wait_ge(sT_, tT[("A", i)])
                            if i >= 2:
                                ve.wait_ge(sS_, tS[("C", i - 2)])
                            ve.scalar_tensor_tensor(v_sbp(0, par), p_s1h[:, :], 1.0,
                                                    v_mask(qt), op0=ALU.mult,
                                                    op1=ALU.add)
                            ve.scalar_tensor_tensor(v_sbp(1, par), p_s2h[:, :], 1.0,
                                                    v_mask(qt), op0=ALU.mult,
                                                    op1=ALU.add)
                            ve.drain()
                            ve.reduce_max(v_row(0, par), v_sbp(0, par), axis=AX.X,
                                          negate=True)
                            bump("sV", ve.reduce_max(v_row(1, par), v_sbp(1, par),
                                                     axis=AX.X, negate=True), tV[it])
                        elif kind == "D":
                            ve.wait_ge(sS_, tS[("C", i)])
                            ve.reciprocal(v_row(4), v_row(3, par))
                            ve.drain()
                            ve.scalar_tensor_tensor(v_row(5), v_row(2, par),
                                                    lamb[:, h:h + 1], v_row(4),
                                                    op0=ALU.mult, op1=ALU.mult)
                            ve.drain()
                            bump("sV", ve.scalar_tensor_tensor(
                                view(AR_B, BF16, 0, 128, SCR + 4 * 2560, 640),
                                v_eb(1, par), v_row(5), v_eb(0, par),
                                op0=ALU.mult, op1=ALU.subtract), tV[it])
                        else:
                            ve.wait_ge(sS_, tS[("E", i)])
                            if i > 0:
                                ve.wait_ge(sT_, tT[("G", i - 1)] - 1)
                            ve.scalar_tensor_tensor(v_row(7, par), v_row(2, par),
                                                    1e-6, v_row(6), op0=ALU.mult,
                                                    op1=ALU.add)
                            ve.drain()
                            ve.reciprocal(v_row(7, par), v_row(7, par))
                            ve.drain()
                            bump("sV", ve.tensor_scalar(v_dfm(), v_sbp(2, par),
                                                        v_row(7, par), None,
                                                        op0=ALU.mult), tV[it])

                for it in order_T:
                    kind, i = it
                    if kind == "A":
                        b, qt = i // 2, i % 2
                        qc = b * 256 + qt * 128
                        w0 = b * 768 + qt * 128

                        @blk.tensor
                        def _(te, g=g, qc=qc, w0=w0, i=i, it=it):
                            te.wait_ge(sV_, tV[("B", i - 1)] if i > 0 else V0)
                            te.matmul(p_s1h[:, 0:512], v_qrb(0, 64, qc, 128),
                                      v_ks(g, w0, 512, 0, 64), start=True, stop=True)
                            te.matmul(p_s1h[:, 512:640], v_qrb(0, 64, qc, 128),
                                      v_ks(g, w0 + 512, 128, 0, 64),
                                      start=True, stop=True)
                            te.matmul(p_s2h[:, 0:512], v_qrb(64, 64, qc, 128),
                                      v_ks(g, w0, 512, 64, 64), start=True, stop=True)
                            bump("sT", te.matmul(p_s2h[:, 512:640],
                                                 v_qrb(64, 64, qc, 128),
                                                 v_ks(g, w0 + 512, 128, 64, 64),
                                                 start=True, stop=True), tT[it])
                    else:
                        b2, qt2 = i // 2, i % 2

                        @blk.tensor
                        def _(te, g=g, i=i, b2=b2, qt2=qt2, it=it):
                            base = tT[it] - 6
                            te.wait_ge(sV_, tV[("F", i)])
                            for j in range(5):
                                if j >= 2:
                                    te.wait_ge(sS_, tS[("H", i)] - 6 + 1 + (j - 2))
                                elif i > 0:
                                    te.wait_ge(sS_, tS[("H", i - 1)] - (1 if j == 0
                                                                       else 2))
                                base += 1
                                bump("sT", te.transpose(
                                    PT4[j % 2][:, :],
                                    view(AR_D, BF16, 0, 128, 21 * 1024 + j * 256, 128),
                                    identb[:, :]), base)
                            te.wait_ge(sS_, tS[("H", i)] - 1)
                            if i > 0:
                                te.wait_ge(sS_, tS[("H", i - 1)])
                            for j in range(4):
                                te.matmul(p_av[:, :],
                                          v_vs(b2 * 6 + qt2 + j, g * 128, 128),
                                          view(AR_D, BF16, 0, 128, 30720 + j * 256, 128),
                                          start=(j == 0), stop=False)
                            bump("sT", te.matmul(
                                p_av[:, :], v_vs(b2 * 6 + qt2 + 4, g * 128, 128),
                                view(AR_D, BF16, 0, 128, 30720 + 4 * 256, 128),
                                start=False, stop=True), tT[it])

                prevm = {"b17": tV[("B", 7)], "e6": tS[("C", 6)]}

        with nc.Block() as blk, \
             nc.psum_tensor("p_oa", [128, 1024], F32) as p_oa, \
             nc.psum_tensor("p_ob", [128, 1024], F32) as p_ob:
            PO = [p_oa, p_ob]
            wo_te = []
            wo_ev = []
            wo_st = []
            WOB = [C.cnt["dW0"], C.cnt["dW1"]]
            for nq in range(16):
                @blk.gpsimd
                def _(gp, nq=nq):
                    if nq < 2:
                        C.wait(gp, "sT")
                    else:
                        gp.wait_ge(C.sem["sT"], wo_te[nq - 2])
                    C.inc("dW%d" % (nq % 2), gp.dma_start(
                        view(AR_B, BF16, 0, 128, 48 * 1024 + (nq % 2) * 4096, 2048),
                        I["wo"][nq].rearrange("p a b -> p (a b)"),
                        max_dma_last_dim=DMA_CAST_MAX))

                @blk.sync
                def _(sy, nq=nq):
                    if nq < 2:
                        C.wait(sy, "sV")
                    else:
                        sy.wait_ge(C.sem["sV"], wo_ev[nq - 2])
                    C.inc("dA", sy.dma_start(v_gt4(nq % 2), s_g[nq]))
                    C.inc("dA", sy.dma_start(v_xt4(nq % 2), I["x"][nq]))

                @blk.tensor
                def _(te, nq=nq):
                    te.wait_ge(C.sem["dW%d" % (nq % 2)],
                               WOB[nq % 2] + 16 * (nq // 2 + 1))
                    if nq < 2:
                        C.wait(te, "sS", "sV")
                    else:
                        te.wait_ge(C.sem["sV"], wo_ev[nq - 2])
                    for tau in range(2):
                        for kc in range(16):
                            ins = te.matmul(PO[nq % 2][:, tau * 512:(tau + 1) * 512],
                                            v_wq4(nq % 2, kc * 128, 128),
                                            v_ao(kc, tau * 512, 512),
                                            start=(kc == 0), stop=(kc == 15))
                    wo_te.append(C.inc("sT", ins))

                @blk.vector
                def _(ve, nq=nq):
                    C.wait(ve, "sT", "dA")
                    if nq < 2:
                        C.wait(ve, "dO")
                    else:
                        ve.wait_ge(C.sem["dO%d" % (nq % 2)], wo_st[nq - 2])
                    ve.tensor_tensor(v_ob(nq % 2), PO[nq % 2][:, :], v_gt4(nq % 2),
                                     op=ALU.mult)
                    ve.drain()
                    wo_ev.append(C.inc("sV", ve.tensor_tensor(v_ob(nq % 2),
                                                              v_ob(nq % 2),
                                                              v_xt4(nq % 2),
                                                              op=ALU.add)))

                @blk.sync
                def _(sy, nq=nq):
                    C.wait(sy, "sV")
                    wo_st.append(C.inc("dO%d" % (nq % 2),
                                       sy.dma_start(out[nq], v_ob(nq % 2))))

            @blk.sync
            def _(sy):
                sy.wait_ge(C.sem["dO"], C.cnt["dO"])
                sy.wait_ge(C.sem["dO0"], C.cnt["dO0"])
                sy.wait_ge(C.sem["dO1"], C.cnt["dO1"])

    return nc



def kernel(**inputs):
    x = np.asarray(inputs["L0_attn_x"], dtype=np.float32)
    shared = _prep_shared(inputs)
    in_maps = [_prep_core(c, x, shared) for c in range(NCORES)]
    nc = build_nc()
    res = run_bass_kernel_spmd(nc, in_maps, core_ids=list(range(NCORES)))
    outs = []
    for c in range(NCORES):
        o = res.results[c]["out"]
        outs.append(o.reshape(D, B, SL).transpose(1, 2, 0))
    return np.concatenate(outs, axis=1).astype(np.float32)


# version 19 (final)
# speedup vs baseline: 1.0290x; 1.0290x over previous
import math
import numpy as np
import ml_dtypes
from contextlib import ExitStack

import concourse.bass as bass
import concourse.mybir as mybir
from concourse.bass_utils import run_bass_kernel_spmd

F32 = mybir.dt.float32
BF16 = mybir.dt.bfloat16
AF = mybir.ActivationFunctionType
ALU = mybir.AluOpType
AX = mybir.AxisListType
BF = ml_dtypes.bfloat16

NCORES = 8
B = 4
S = 2048
D = 2048
SL = S // NCORES
NT = B * SL
HW = 512
KTB = HW + SL
HD = 128
NE = 8
NEG = -1.0e30
EPS = 1e-5
DMA_CAST_MAX = 8192


def _f(a):
    return np.ascontiguousarray(a, dtype=np.float32)


def _rope_tables(positions, scale=1.0):
    pos = np.asarray(positions, dtype=np.float64)
    inv = 1.0 / (10000.0 ** (np.arange(0, HD, 2, dtype=np.float64) / HD))
    ang = np.concatenate([inv, inv])[:, None] * pos[None, :]
    out = []
    for t in (np.cos(ang) * scale, np.sin(ang) * scale):
        hi = t.astype(np.float32)
        lo = (t - hi.astype(np.float64)).astype(np.float32)
        out += [hi, lo]
    return out


def _prep_shared(inp):
    sh = {}
    r4 = lambda w, K, N: np.ascontiguousarray(
        np.asarray(w, dtype=np.float32).reshape(K // 128, 128, N // 128, 128)
        .transpose(2, 1, 0, 3))
    sh["wq"] = r4(inp["L0_attn_Wq"], 2048, 2048)
    wqf = np.asarray(inp["L0_attn_Wq"], np.float32)
    perm = np.concatenate([np.arange(64, 128), np.arange(0, 64)])
    wq_p = wqf.reshape(2048, 16, 128)[:, :, perm].reshape(2048, 2048)
    sh["wqp"] = r4(wq_p, 2048, 2048)
    sh["wk"] = r4(inp["L0_attn_Wk"], 2048, 512)
    sh["wvr"] = _f(np.asarray(inp["L0_attn_Wv"], np.float32)
                   .reshape(16, 128, 512).transpose(1, 0, 2))
    sh["wo"] = r4(inp["L0_attn_Wo"], 2048, 2048)
    sh["wqkv"] = r4(np.asarray(inp["L0_ffn_qkv_weight"], np.float32).T, 2048, 3072)
    sh["wproj"] = r4(np.asarray(inp["L0_ffn_proj_weight"], np.float32).T, 2048, 2048)
    sh["wrout"] = _f(np.asarray(inp["L0_ffn_router_weight"], np.float32)
                     .reshape(16, 128, 8).transpose(1, 0, 2).reshape(128, 128))
    w1 = np.asarray(inp["L0_ffn_moe_w1"], np.float32)
    sh["w1"] = np.ascontiguousarray(
        w1.reshape(NE, 16, 128, 32, 128).transpose(0, 3, 2, 1, 4))
    w2 = np.asarray(inp["L0_ffn_moe_w2"], np.float32)
    sh["w2"] = np.ascontiguousarray(
        w2.reshape(NE, 2, 16, 128, 16, 128).transpose(0, 1, 4, 3, 2, 5))
    for nm, key in [("l1w", "L0_ffn_ln1_weight"), ("l1b", "L0_ffn_ln1_bias"),
                    ("l2w", "L0_ffn_ln2_weight"), ("l2b", "L0_ffn_ln2_bias")]:
        sh[nm] = _f(np.asarray(inp[key], np.float32).reshape(16, 128).T)
    sh["lam"] = _f(np.asarray(inp["L0_attn_lam"], np.float32).reshape(1, 16))
    sh["identb"] = np.ascontiguousarray(np.eye(128, dtype=np.float32).astype(BF))
    rot = np.zeros((128, 128), dtype=np.float32)
    for m in range(64):
        rot[m + 64, m] = -1.0
        rot[m, m + 64] = 1.0
    sh["rotb"] = np.ascontiguousarray(rot.astype(BF))
    sh["identf"] = _f(np.eye(128, dtype=np.float32))
    oh = np.zeros((8, 1024), dtype=np.float32)
    for e in range(8):
        oh[e, e * 128:(e + 1) * 128] = 1.0
    sh["oneh8"] = _f(oh)
    sh["onesf"] = _f(np.ones((1, 128), dtype=np.float32))
    sh["onesb"] = np.ascontiguousarray(np.ones((128, 1), np.float32).astype(BF))
    return sh


def _prep_core(core, x, shared):
    s0 = SL * core
    d = dict(shared)
    xr = np.asarray(x[:, s0:s0 + SL, :], dtype=np.float32)
    d["x"] = _f(xr.transpose(2, 0, 1).reshape(16, 128, NT))
    xh = np.zeros((B, HW, D), dtype=np.float32)
    lo = max(0, s0 - HW)
    n = s0 - lo
    if n > 0:
        xh[:, HW - n:, :] = x[:, lo:s0, :]
    d["xh"] = _f(xh.reshape(B, 2, 256, D).transpose(0, 1, 3, 2)
                 .reshape(B * 2, 16, 128, 256))
    kpos = np.maximum(0, np.arange(s0 - HW, s0 + SL))
    qpos = np.tile(np.arange(s0, s0 + SL), B)
    ch, cl, sh_, sl_ = _rope_tables(kpos)
    d["ckh"], d["ckl"], d["skh"], d["skl"] = _f(ch), _f(cl), _f(sh_), _f(sl_)
    ch, cl, sh_, sl_ = _rope_tables(qpos, scale=1.0 / math.sqrt(HD // 2))
    sgn = np.concatenate([-np.ones(64), np.ones(64)]).astype(np.float32)[:, None]
    d["cqh"], d["cql"] = _f(ch), _f(cl)
    d["sqh"], d["sql"] = _f(sh_ * sgn), _f(sl_ * sgn)
    masks = np.zeros((2, 128, 640), dtype=np.float32)
    i = np.arange(128)[:, None]
    j = np.arange(640)[None, :]
    for qt in range(2):
        ok = (j - i >= 1) & (j - i <= HW) & (s0 + 128 * qt - HW + j >= 0)
        masks[qt] = np.where(ok, 0.0, NEG)
    d["mask"] = _f(masks)
    return d



class Ctx:
    def __init__(self, nc, es):
        self.nc = nc
        self.sem = {}
        self.cnt = {}
        for name in ("dW", "dW0", "dW1", "dW2", "dW3", "dW4", "dA", "dO",
                     "dO0", "dO1", "sT", "sV", "sS", "sG"):
            self.sem[name] = es.enter_context(nc.semaphore(name))
            self.cnt[name] = 0

    def inc(self, name, instr, n=None):
        n = n if n is not None else (16 if name.startswith("d") else 1)
        instr.then_inc(self.sem[name], n)
        self.cnt[name] += n
        return self.cnt[name]

    def wait(self, eng, *names, **kw):
        for name in names:
            v = kw.get("val", self.cnt[name])
            if v > 0:
                eng.wait_ge(self.sem[name], v)


def view(t, dtype, p0, np_, byte0, ncols):
    esz = mybir.dt.size(dtype)
    assert byte0 % esz == 0
    return t.bitcast(dtype)[p0:p0 + np_, byte0 // esz: byte0 // esz + ncols]


DBG = None


def build_nc():
    nc = bass.Bass("TRN2", target_bir_lowering=False)
    I = {}
    def di(name, shape, dt=F32):
        I[name] = nc.dram_tensor(name, list(shape), dt, kind="ExternalInput")
    di("x", (16, 128, NT)); di("xh", (8, 16, 128, 256))
    for nm in ("ckh", "ckl", "skh", "skl"):
        di(nm, (128, KTB))
    for nm in ("cqh", "cql", "sqh", "sql"):
        di(nm, (128, NT))
    di("mask", (2, 128, 640))
    di("wq", (16, 128, 16, 128)); di("wqp", (16, 128, 16, 128))
    di("wk", (4, 128, 16, 128)); di("wvr", (128, 16, 512))
    di("wo", (16, 128, 16, 128)); di("wqkv", (24, 128, 16, 128)); di("wproj", (16, 128, 16, 128))
    di("wrout", (128, 128))
    di("w1", (NE, 32, 128, 16, 128)); di("w2", (NE, 2, 16, 128, 16, 128))
    di("l1w", (128, 16)); di("l1b", (128, 16)); di("l2w", (128, 16)); di("l2b", (128, 16))
    di("lam", (1, 16))
    di("identb", (128, 128), BF16); di("identf", (128, 128))
    di("rotb", (128, 128), BF16)
    di("oneh8", (8, 1024)); di("onesf", (1, 128)); di("onesb", (128, 1), BF16)
    out = nc.dram_tensor("out", [16, 128, NT], F32, kind="ExternalOutput")

    s_xn16 = nc.dram_tensor("s_xn16", [4, 128, 16 * 256], BF16)
    s_k = nc.dram_tensor("s_k", [128, 4 * 3072], BF16)
    s_v = nc.dram_tensor("s_v", [128, 24 * 512], BF16)
    s_g = nc.dram_tensor("s_g", [16, 128, NT], F32)

    dbg = {}
    _dbg_specs = {
        "p1": [("dxn32", [128, 16 * NT], F32), ("dxn16", [4, 128, 16 * 256], BF16),
               ("dk", [128, 4 * 3072], BF16), ("dv", [128, 24 * 512], BF16),
               ("dl1s", [128, 2048], F32)],
        "p2a": [("dln1", [128, 16 * NT], BF16), ("dqkv", [128, 24 * NT], BF16)],
        "p2b": [("dao2", [128, 16 * NT], BF16)],
        "p2c": [("dh2", [128, 16 * NT], F32), ("dxmoe", [128, 16 * NT], BF16),
                ("dgfm", [8, NT], F32)],
        "p3": [("dg", [16, 128, NT], F32)],
    }
    if DBG in _dbg_specs:
        for nm, shape, dt in _dbg_specs[DBG]:
            dbg[nm] = nc.dram_tensor(nm, shape, dt, kind="ExternalOutput")

    es = ExitStack()
    with es:
        C = Ctx(nc, es)
        sbt = lambda n, s, d: es.enter_context(nc.sbuf_tensor(n, s, d))
        AR_A = sbt("AR_A", [128, 44 * 1024], mybir.dt.uint8)
        AR_B = sbt("AR_B", [128, 64 * 1024], mybir.dt.uint8)
        AR_C = sbt("AR_C", [128, 48 * 1024], mybir.dt.uint8)
        AR_D = sbt("AR_D", [128, 32 * 1024], mybir.dt.uint8)
        identb = sbt("c_identb", [128, 128], BF16)
        rotb = sbt("c_rotb", [128, 128], BF16)
        identf = sbt("c_identf", [128, 128], F32)
        onesf = sbt("c_onesf", [1, 128], F32)
        onesb = sbt("c_onesb", [128, 1], BF16)
        lamb = sbt("lamb", [128, 16], F32)
        l1w = sbt("c_l1w", [128, 16], F32); l1b = sbt("c_l1b", [128, 16], F32)
        l2w = sbt("c_l2w", [128, 16], F32); l2b = sbt("c_l2b", [128, 16], F32)
        wroutb = sbt("wroutb", [128, 128], BF16)
        lam1 = sbt("lam1", [1, 16], F32)
        epsb = sbt("epsb", [128, 1], F32)
        R = sbt("R", [128, 1536], F32)
        R2 = sbt("R2", [128, 2048], F32)

        def v_cosk(p0, np_, c0, nn):
            return view(AR_D, F32, p0, np_, 24 * 1024 + c0 * 4, nn)

        def v_sink(p0, np_, c0, nn):
            return view(AR_D, F32, p0, np_, 27 * 1024 + 72 + c0 * 4, nn)

        with nc.Block() as blk:
            @blk.sync
            def _(sy):
                for dst, src in [(identb, I["identb"]), (rotb, I["rotb"]),
                                 (identf, I["identf"]),
                                 (onesf, I["onesf"]), (onesb, I["onesb"]),
                                 (l1w, I["l1w"]), (l1b, I["l1b"]),
                                 (l2w, I["l2w"]), (l2b, I["l2b"]),
                                 (lam1, I["lam"])]:
                    C.inc("dA", sy.dma_start(dst[:, :], src[:, :]))
                C.inc("dA", sy.dma_start(v_cosk(0, 128, 0, KTB), I["ckh"][:, :]))
                C.inc("dA", sy.dma_start(v_sink(0, 128, 0, KTB), I["skh"][:, :]))
                C.inc("dA", sy.dma_start(view(AR_A, F32, 0, 128, 0, KTB), I["ckl"][:, :]))
                C.inc("dA", sy.dma_start(view(AR_A, F32, 0, 128, 4 * KTB, KTB), I["skl"][:, :]))

            @blk.gpsimd
            def _(gp):
                C.inc("dW", gp.dma_start(wroutb[:, :], I["wrout"][:, :],
                                         max_dma_last_dim=DMA_CAST_MAX))

            @blk.vector
            def _(ve):
                ve.memset(epsb[:, :], EPS)
                ve.memset(R[:, :], 0.0)
                ve.memset(R2[:, :], 0.0)
                C.wait(ve, "dA")
                ve.tensor_tensor(v_cosk(0, 128, 0, KTB), v_cosk(0, 128, 0, KTB),
                                 view(AR_A, F32, 0, 128, 0, KTB), op=ALU.add)
                ve.tensor_tensor(v_sink(0, 128, 0, KTB), v_sink(0, 128, 0, KTB),
                                 view(AR_A, F32, 0, 128, 4 * KTB, KTB), op=ALU.add)

        with nc.Block() as blk, nc.psum_tensor("pb0", [128, 512], F32) as pb0:
            @blk.tensor
            def _(te):
                C.wait(te, "dA")
                C.inc("sT", te.matmul(pb0[:, 0:16], onesf[:, :], lam1[:, :],
                                      start=True, stop=True))

            @blk.scalar
            def _(sc):
                C.wait(sc, "sT")
                C.inc("sS", sc.copy(lamb[:, :], pb0[:, 0:16]))

        def v_xnslab(c, t0=0, nn=256):
            return view(AR_A, BF16, 0, 128, (c * 256 + t0) * 2, nn)

        def v_xsl(c, t0=0, nn=256):
            return view(AR_A, F32, 0, 128, 8 * 1024 + (c * 256 + t0) * 4, nn)

        def v_xbuf(c, nn=256):
            return view(AR_A, BF16, 0, 128, 24 * 1024 + c * 512, nn)

        def v_sqb(c, nn=256):
            return view(AR_A, BF16, 0, 128, 32 * 1024 + c * 512, nn)

        def v_tmp(slot, p0=0, np_=128, nn=256):
            return view(AR_A, F32, p0, np_, 40 * 1024 + slot * 1024, nn)

        def v_xn32(c, t0=0, nn=NT):
            return view(AR_B, F32, 0, 128, (c * NT + t0) * 4, nn)

        def v_ks(g, k0, nn, p0=0, np_=128):
            return view(AR_C, BF16, p0, np_, (g * 3072 + k0) * 2, nn)

        def v_vs(tile, c0=0, nn=512):
            return view(AR_C, BF16, 0, 128, 24 * 1024 + (tile * 512 + c0) * 2, nn)

        def v_wvb(kc):
            return view(AR_D, BF16, 0, 128, kc * 1024, 512)

        def v_wkb(par, kc):
            return view(AR_D, BF16, 0, 128, 16 * 1024 + par * 4096 + kc * 256, 128)

        slabs = [("h", hb) for hb in range(8)] + [("m", b) for b in range(4)]

        with nc.Block() as blk, \
             nc.psum_tensor("p_s1", [1, 256], F32) as p_s1, \
             nc.psum_tensor("p_s2", [1, 256], F32) as p_s2, \
             nc.psum_tensor("p_bc", [128, 512], F32) as p_bc, \
             nc.psum_tensor("p_kv", [128, 512], F32) as p_kv, \
             nc.psum_tensor("p_kv2", [128, 512], F32) as p_kv2, \
             nc.psum_tensor("p_rot", [128, 256], F32) as p_rot, \
             nc.psum_tensor("p_rot2", [128, 256], F32) as p_rot2:

            @blk.gpsimd
            def _(gp):
                C.inc("dW", gp.dma_start(view(AR_D, BF16, 0, 128, 0, 16 * 512),
                                         I["wvr"].rearrange("p a b -> p (a b)"),
                                         max_dma_last_dim=DMA_CAST_MAX))

            sv_applydone = [0] * len(slabs)
            ss_lastread_xsl = [0] * len(slabs)

            for si, (kind, idx) in enumerate(slabs):
                main = kind == "m"
                b = idx // 2 if kind == "h" else idx
                half = idx % 2 if kind == "h" else None
                o0 = half * 256 if kind == "h" else 512
                kcol0 = b * 768 + o0
                vt0 = b * 6 + (half * 2 if kind == "h" else 4)

                @blk.sync
                def _(sy, si=si, kind=kind, idx=idx):
                    if si >= 1:
                        sy.wait_ge(C.sem["sV"], sv_applydone[si - 1])
                        sy.wait_ge(C.sem["sS"], ss_lastread_xsl[si - 1])
                    for c in range(16):
                        if kind == "h":
                            C.inc("dA", sy.dma_start(v_xsl(c), I["xh"][idx, c, :, :]))
                        else:
                            C.inc("dA", sy.dma_start(
                                v_xsl(c), I["x"][c, :, idx * 256:(idx + 1) * 256]))

                @blk.vector
                def _(ve):
                    C.wait(ve, "dA")
                    for c in range(15):
                        ve.tensor_copy(v_xbuf(c), v_xsl(c))
                    C.inc("sV", ve.tensor_copy(v_xbuf(15), v_xsl(15)))

                @blk.scalar
                def _(sc):
                    C.wait(sc, "dA")
                    for c in range(15):
                        sc.activation(v_sqb(c), v_xsl(c), AF.Square)
                    C.inc("sS", sc.activation(v_sqb(15), v_xsl(15), AF.Square))

                @blk.tensor
                def _(te):
                    C.wait(te, "sV", "sS")
                    for c in range(16):
                        te.matmul(p_s1[:, :], onesb[:, :], v_xbuf(c),
                                  start=(c == 0), stop=(c == 15))
                    for c in range(15):
                        te.matmul(p_s2[:, :], onesb[:, :], v_sqb(c),
                                  start=(c == 0), stop=False)
                    C.inc("sT", te.matmul(p_s2[:, :], onesb[:, :], v_sqb(15),
                                          start=False, stop=True))

                @blk.vector
                def _(ve):
                    C.wait(ve, "sT")
                    ve.tensor_scalar(R[0:1, 0:256], p_s1[:, :], 1.0 / D, None,
                                     op0=ALU.mult)
                    ve.drain()
                    ve.tensor_tensor(R[0:1, 768:1024], R[0:1, 0:256], R[0:1, 0:256],
                                     op=ALU.mult)
                    ve.drain()
                    C.inc("sV", ve.scalar_tensor_tensor(
                        R[0:1, 768:1024], p_s2[:, :], 1.0 / D, R[0:1, 768:1024],
                        op0=ALU.mult, op1=ALU.subtract))

                @blk.scalar
                def _(sc):
                    C.wait(sc, "sV")
                    sc.activation(R[0:1, 256:512], R[0:1, 768:1024], AF.Ln,
                                  bias=epsb[0:1, :])
                    sc.drain()
                    C.inc("sS", sc.activation(R[0:1, 256:512], R[0:1, 256:512], AF.Exp,
                                              scale=-0.5))

                @blk.vector
                def _(ve):
                    C.wait(ve, "sS")
                    C.inc("sV", ve.scalar_tensor_tensor(
                        R[0:1, 512:768], R[0:1, 0:256], -1.0, R[0:1, 256:512],
                        op0=ALU.mult, op1=ALU.mult))

                @blk.tensor
                def _(te):
                    C.wait(te, "sV", "sS")
                    te.matmul(p_bc[:, 0:256], onesf[:, :], R[0:1, 256:512],
                              start=True, stop=True)
                    C.inc("sT", te.matmul(p_bc[:, 256:512], onesf[:, :],
                                          R[0:1, 512:768], start=True, stop=True))

                @blk.vector
                def _(ve, main=main, b=b, si=si):
                    C.wait(ve, "sT", "dA")
                    for c in range(16):
                        ve.tensor_tensor(v_xsl(c), v_xsl(c), p_bc[:, 0:256], op=ALU.mult)
                    ve.drain()
                    for c in range(16):
                        dst = v_xn32(c, b * 256, 256) if main else v_xnslab(c)
                        ins = ve.tensor_tensor(dst, v_xsl(c), p_bc[:, 256:512], op=ALU.add)
                        if c == 15:
                            sv_applydone[si] = C.inc("sV", ins)

                if main:
                    @blk.scalar
                    def _(sc, b=b, si=si):
                        C.wait(sc, "sV")
                        for c in range(16):
                            sc.copy(v_xnslab(c), v_xn32(c, b * 256, 256))
                        for c in range(15):
                            sc.activation(v_sqb(c), v_xn32(c, b * 256, 256), AF.Square)
                        ss_lastread_xsl[si] = C.inc(
                            "sS", sc.activation(v_sqb(15), v_xn32(15, b * 256, 256),
                                                AF.Square))
                else:
                    ss_lastread_xsl[si] = C.cnt["sS"]

                if main:
                    @blk.tensor
                    def _(te):
                        C.wait(te, "sS")
                        for c in range(16):
                            te.matmul(p_s1[:, :], onesb[:, :], v_xnslab(c),
                                      start=(c == 0), stop=(c == 15))
                        for c in range(15):
                            te.matmul(p_s2[:, :], onesb[:, :], v_sqb(c),
                                      start=(c == 0), stop=False)
                        C.inc("sT", te.matmul(p_s2[:, :], onesb[:, :], v_sqb(15),
                                              start=False, stop=True))

                    @blk.vector
                    def _(ve, b=b):
                        C.wait(ve, "sT")
                        ve.tensor_scalar(R[0:1, 1024:1280], p_s1[:, :], 1.0 / D, None,
                                         op0=ALU.mult)
                        ve.drain()
                        ve.tensor_tensor(R[0:1, 1280:1536], R[0:1, 1024:1280],
                                         R[0:1, 1024:1280], op=ALU.mult)
                        ve.drain()
                        C.inc("sV", ve.scalar_tensor_tensor(
                            R[0:1, 1280:1536], p_s2[:, :], 1.0 / D, R[0:1, 1280:1536],
                            op0=ALU.mult, op1=ALU.subtract))

                    @blk.scalar
                    def _(sc, b=b):
                        C.wait(sc, "sV")
                        sc.activation(R2[0:1, b * 256:(b + 1) * 256],
                                      R[0:1, 1280:1536], AF.Ln, bias=epsb[0:1, :])
                        sc.drain()
                        C.inc("sS", sc.activation(
                            R2[0:1, b * 256:(b + 1) * 256],
                            R2[0:1, b * 256:(b + 1) * 256], AF.Exp, scale=-0.5))

                    @blk.vector
                    def _(ve, b=b):
                        C.wait(ve, "sS")
                        C.inc("sV", ve.scalar_tensor_tensor(
                            R2[0:1, 1024 + b * 256:1024 + (b + 1) * 256],
                            R[0:1, 1024:1280], -1.0,
                            R2[0:1, b * 256:(b + 1) * 256],
                            op0=ALU.mult, op1=ALU.mult))

                    @blk.sync
                    def _(sy, b=b):
                        C.wait(sy, "sS")
                        C.inc("dA", sy.dma_start(
                            s_xn16[b, :, :],
                            view(AR_A, BF16, 0, 128, 0, 16 * 256)))

                @blk.tensor
                def _(te, vt0=vt0):
                    C.wait(te, "sS", "dW")
                    C.wait(te, "sV")
                    for kc in range(15):
                        te.matmul(p_kv[:, :], v_xnslab(kc, 0, 128), v_wvb(kc),
                                  start=(kc == 0), stop=False)
                    C.inc("sT", te.matmul(p_kv[:, :], v_xnslab(15, 0, 128), v_wvb(15),
                                          start=False, stop=True))
                    for kc in range(15):
                        te.matmul(p_kv2[:, :], v_xnslab(kc, 128, 128), v_wvb(kc),
                                  start=(kc == 0), stop=False)
                    C.inc("sT", te.matmul(p_kv2[:, :], v_xnslab(15, 128, 128), v_wvb(15),
                                          start=False, stop=True))

                vp_box = []

                @blk.vector
                def _(ve, vt0=vt0, vp_box=vp_box):
                    ve.wait_ge(C.sem["sT"], C.cnt["sT"] - 1)
                    ve.tensor_copy(v_vs(vt0), p_kv[:, :])
                    C.wait(ve, "sT")
                    vp_box.append(C.inc("sV", ve.tensor_copy(v_vs(vt0 + 1),
                                                             p_kv2[:, :])))

                PROT = [p_rot, p_rot2]
                kv_ev = []
                for g in range(4):
                    @blk.gpsimd
                    def _(gp, g=g):
                        C.wait(gp, "sT")
                        C.inc("dW", gp.dma_start(
                            view(AR_D, BF16, 0, 128, 16 * 1024 + (g % 2) * 4096, 2048),
                            I["wk"][g].rearrange("p a b -> p (a b)"),
                            max_dma_last_dim=DMA_CAST_MAX))

                    @blk.tensor
                    def _(te, g=g, vp_box=vp_box):
                        C.wait(te, "dW", "sS")
                        if g < 2:
                            te.wait_ge(C.sem["sV"], vp_box[0])
                        else:
                            te.wait_ge(C.sem["sV"], kv_ev[g - 2])
                        pk = p_kv if g % 2 == 0 else p_kv2
                        for kc in range(15):
                            te.matmul(pk[:, 0:256], v_wkb(g % 2, kc), v_xnslab(kc),
                                      start=(kc == 0), stop=False)
                        C.inc("sT", te.matmul(pk[:, 0:256], v_wkb(g % 2, 15),
                                              v_xnslab(15), start=False, stop=True))

                    @blk.vector
                    def _(ve, g=g):
                        C.wait(ve, "sT")
                        pk = p_kv if g % 2 == 0 else p_kv2
                        kv_ev.append(C.inc("sV", ve.tensor_copy(
                            view(AR_A, BF16, 0, 128, 40 * 1024 + (g % 2) * 3072, 256),
                            pk[:, 0:256])))

                    @blk.tensor
                    def _(te, g=g):
                        te.wait_ge(C.sem["sV"], kv_ev[g])
                        C.inc("sT", te.matmul(
                            PROT[g % 2][:, :], rotb[:, :],
                            view(AR_A, BF16, 0, 128, 40 * 1024 + (g % 2) * 3072, 256),
                            start=True, stop=True))

                    @blk.vector
                    def _(ve, g=g, o0=o0, kcol0=kcol0):
                        C.wait(ve, "sT")
                        ve.tensor_tensor(v_tmp(1),
                                         view(AR_A, BF16, 0, 128,
                                              40 * 1024 + (g % 2) * 3072, 256),
                                         v_cosk(0, 128, o0, 256), op=ALU.mult)
                        ve.tensor_tensor(v_tmp(2), PROT[g % 2][:, :],
                                         v_sink(0, 128, o0, 256), op=ALU.mult)
                        ve.drain()
                        C.inc("sV", ve.tensor_tensor(
                            v_ks(g, kcol0, 256), v_tmp(1), v_tmp(2), op=ALU.add))

            @blk.sync
            def _(sy):
                C.wait(sy, "sV")
                C.wait(sy, "dA")
                C.inc("dA", sy.dma_start(s_k[:, :], view(AR_C, BF16, 0, 128, 0, 4 * 3072)))
                C.inc("dA", sy.dma_start(s_v[:, :], view(AR_C, BF16, 0, 128, 24 * 1024, 24 * 512)))
                if DBG == "p1":
                    C.inc("dO", sy.dma_start(dbg["dxn32"][:, :],
                                             view(AR_B, F32, 0, 128, 0, 16 * NT)))
                    C.inc("dO", sy.dma_start(dbg["dk"][:, :],
                                             view(AR_C, BF16, 0, 128, 0, 4 * 3072)))
                    C.inc("dO", sy.dma_start(dbg["dv"][:, :],
                                             view(AR_C, BF16, 0, 128, 24 * 1024, 24 * 512)))
                    C.inc("dO", sy.dma_start(dbg["dl1s"][:, :], R2[:, 0:2048]))
                    for bb in range(4):
                        C.inc("dO", sy.dma_start(dbg["dxn16"][bb], s_xn16[bb]))
                    sy.wait_ge(C.sem["dO"], C.cnt["dO"])

        if DBG == "p1":
            return nc

        def v_ln1(c, t0=0, nn=NT):
            return view(AR_A, BF16, 0, 128, (c * NT + t0) * 2, nn)

        def v_wslab(par, c0, nn):
            return view(AR_A, BF16, 0, 128, 32 * 1024 + par * 4096 + c0 * 2, nn)

        def v_lscr(t0=0, nn=NT):
            return view(AR_A, F32, 0, 128, 40 * 1024 + t0 * 4, nn)

        def v_qkv(cc, t0=0, nn=NT):
            return view(AR_C, BF16, 0, 128, (cc * NT + t0) * 2, nn)

        with nc.Block() as blk, \
             nc.psum_tensor("p_r1", [128, 1024], F32) as p_r1, \
             nc.psum_tensor("p_m1", [128, 1024], F32) as p_m1, \
             nc.psum_tensor("p_mma", [128, 1024], F32) as p_mma, \
             nc.psum_tensor("p_mmb", [128, 1024], F32) as p_mmb:

            @blk.tensor
            def _(te):
                C.wait(te, "sV", "sS", "dA")
                te.matmul(p_r1[:, 0:512], onesf[:, :], R2[0:1, 0:512], start=True, stop=True)
                te.matmul(p_r1[:, 512:1024], onesf[:, :], R2[0:1, 512:1024], start=True, stop=True)
                te.matmul(p_m1[:, 0:512], onesf[:, :], R2[0:1, 1024:1536], start=True, stop=True)
                C.inc("sT", te.matmul(p_m1[:, 512:1024], onesf[:, :], R2[0:1, 1536:2048],
                                      start=True, stop=True))

            @blk.vector
            def _(ve):
                C.wait(ve, "sT")
                for c in range(16):
                    ve.tensor_tensor(v_lscr(), v_xn32(c), p_r1[:, :], op=ALU.mult)
                    ve.drain()
                    ve.tensor_tensor(v_lscr(), v_lscr(), p_m1[:, :], op=ALU.add)
                    ve.drain()
                    ins = ve.tensor_scalar(v_ln1(c), v_lscr(), l1w[:, c:c + 1],
                                           l1b[:, c:c + 1], op0=ALU.mult, op1=ALU.add)
                    if c == 15:
                        C.inc("sV", ins)
                    else:
                        ve.drain()

            qkv_ev = []
            qkv_te = []
            WQB = [C.cnt["dW0"], C.cnt["dW1"]]
            for nq in range(24):
                @blk.gpsimd
                def _(gp, nq=nq):
                    if nq < 2:
                        C.wait(gp, "sT")
                    else:
                        gp.wait_ge(C.sem["sT"], qkv_te[nq - 2])
                    C.inc("dW%d" % (nq % 2), gp.dma_start(
                        view(AR_A, BF16, 0, 128, 32 * 1024 + (nq % 2) * 4096, 2048),
                        I["wqkv"][nq].rearrange("p a b -> p (a b)"),
                        max_dma_last_dim=DMA_CAST_MAX))

                @blk.tensor
                def _(te, nq=nq):
                    te.wait_ge(C.sem["dW%d" % (nq % 2)],
                               WQB[nq % 2] + 16 * (nq // 2 + 1))
                    if nq == 0:
                        C.wait(te, "sV", "sS")
                    elif nq >= 2:
                        te.wait_ge(C.sem["sV"], qkv_ev[nq - 2])
                    pm = p_mma if nq % 2 == 0 else p_mmb
                    for tau in range(2):
                        for kc in range(16):
                            ins = te.matmul(pm[:, tau * 512:(tau + 1) * 512],
                                            v_wslab(nq % 2, kc * 128, 128),
                                            v_ln1(kc, tau * 512, 512),
                                            start=(kc == 0), stop=(kc == 15))
                    qkv_te.append(C.inc("sT", ins))

                @blk.vector
                def _(ve, nq=nq):
                    C.wait(ve, "sT")
                    pm = p_mma if nq % 2 == 0 else p_mmb
                    ve.tensor_copy(v_qkv(nq, 0, 512), pm[:, 0:512])
                    qkv_ev.append(C.inc("sV", ve.tensor_copy(v_qkv(nq, 512, 512),
                                                             pm[:, 512:1024])))

        if DBG == "p2a":
            with nc.Block() as blk:
                @blk.sync
                def _(sy):
                    C.wait(sy, "sS", "sV")
                    C.inc("dO", sy.dma_start(dbg["dln1"][:, :],
                                             view(AR_A, BF16, 0, 128, 0, 16 * NT)))
                    C.inc("dO", sy.dma_start(dbg["dqkv"][:, :],
                                             view(AR_C, BF16, 0, 128, 0, 24 * NT)))
                    sy.wait_ge(C.sem["dO"], C.cnt["dO"])
            return nc

        def v_qtm(h, c0=0, nn=128):
            return view(AR_A, BF16, 0, 128, (h * 128 + c0) * 2, nn)

        def v_krep(bt, h, nn=128):
            return view(AR_A, BF16, 0, 128, 4 * 1024 + (bt * 2048 + h * 128) * 2, nn)

        def v_vtm(bt, g, nn=128):
            return view(AR_A, BF16, 0, 128, 20 * 1024 + (bt * 512 + g * 128) * 2, nn)

        def v_pf(c0=0, nn=2048):
            return view(AR_A, BF16, 0, 128, 24 * 1024 + c0 * 2, nn)

        def v_acc(c0=0, nn=2048):
            return view(AR_A, F32, 0, 128, 28 * 1024 + c0 * 4, nn)

        def v_accb(c0=0, nn=2048):
            return view(AR_A, BF16, 0, 128, 36 * 1024 + c0 * 2, nn)

        def v_sp(i, nn=16):
            return view(AR_A, F32, 0, 128, 40 * 1024 + i * 64, nn)

        def v_ao2(h, t0=0, nn=NT):
            return view(AR_D, BF16, 0, 128, (h * NT + t0) * 2, nn)

        S128 = 1.0 / math.sqrt(128.0)

        with nc.Block() as blk, \
             nc.psum_tensor("ptr0", [128, 128], BF16) as ptr0, \
             nc.psum_tensor("ptr1", [128, 128], BF16) as ptr1:
            PTR = [ptr0, ptr1]
            for ht in range(2):
                for bt in range(4):
                    t0 = bt * 256 + ht * 128
                    for g in range(4):
                        @blk.tensor
                        def _(te, bt=bt, g=g, t0=t0):
                            C.wait(te, "sS", "sV")
                            te.transpose(PTR[0][:, :], v_qkv(6 * g + 4, t0, 128),
                                         identb[:, :])
                            C.inc("sT", te.transpose(PTR[1][:, :],
                                                     v_qkv(6 * g + 5, t0, 128),
                                                     identb[:, :]))

                        @blk.scalar
                        def _(sc, bt=bt, g=g):
                            C.wait(sc, "sT")
                            sc.copy(v_krep(bt, 4 * g), PTR[0][:, :])
                            C.inc("sS", sc.copy(v_krep(bt, 4 * g + 1), PTR[0][:, :]))

                        @blk.vector
                        def _(ve, bt=bt, g=g):
                            C.wait(ve, "sT")
                            ve.tensor_copy(v_krep(bt, 4 * g + 2), PTR[0][:, :])
                            ve.tensor_copy(v_krep(bt, 4 * g + 3), PTR[0][:, :])
                            C.inc("sV", ve.tensor_copy(v_vtm(bt, g), PTR[1][:, :]))

                for bs in range(4):
                    t0 = bs * 256 + ht * 128
                    for h in range(16):
                        @blk.tensor
                        def _(te, h=h, t0=t0):
                            if h == 0:
                                C.wait(te, "sS", "sV")
                            else:
                                C.wait(te, "sS" if h % 2 == 0 else "sV")
                            cc = 6 * (h // 4) + (h % 4)
                            C.inc("sT", te.transpose(PTR[h % 2][:, :],
                                                     v_qkv(cc, t0, 128), identb[:, :]))

                        if h % 2 == 0:
                            @blk.scalar
                            def _(sc, h=h):
                                C.wait(sc, "sT")
                                C.inc("sS", sc.copy(v_qtm(h), PTR[0][:, :]))
                        else:
                            @blk.vector
                            def _(ve, h=h):
                                C.wait(ve, "sT")
                                C.inc("sV", ve.tensor_copy(v_qtm(h), PTR[1][:, :]))

                    @blk.vector
                    def _(ve, bs=bs):
                        C.wait(ve, "sS")
                        ve.drain()
                        for bt in range(bs + 1):
                            ve.tensor_tensor(v_pf(), v_qtm(0, 0, 2048),
                                             v_krep(bt, 0, 2048), op=ALU.mult)
                            ve.drain()
                            ve.reduce_sum(
                                v_sp(bt),
                                v_pf().rearrange("p (a b) -> p a b", b=128), axis=AX.X)
                            ve.drain()
                        ve.tensor_copy(v_sp(8), v_sp(0))
                        for bt in range(1, bs + 1):
                            ve.drain()
                            ve.tensor_tensor(v_sp(8), v_sp(8), v_sp(bt), op=ALU.max)
                        ve.drain()
                        ve.tensor_scalar(v_sp(9), v_sp(8), -S128, None, op0=ALU.mult)
                        ve.drain()
                        for bt in range(bs + 1):
                            ins = ve.scalar_tensor_tensor(
                                v_sp(4 + bt), v_sp(bt), S128, v_sp(9),
                                op0=ALU.mult, op1=ALU.add)
                        C.inc("sV", ins)

                    @blk.scalar
                    def _(sc, bs=bs):
                        C.wait(sc, "sV")
                        for bt in range(bs + 1):
                            ins = sc.activation(v_sp(4 + bt), v_sp(4 + bt), AF.Exp)
                        C.inc("sS", ins)

                    @blk.vector
                    def _(ve, bs=bs, t0=t0):
                        C.wait(ve, "sS")
                        ve.tensor_copy(v_sp(10), v_sp(4))
                        for bt in range(1, bs + 1):
                            ve.drain()
                            ve.tensor_tensor(v_sp(10), v_sp(10), v_sp(4 + bt), op=ALU.add)
                        ve.drain()
                        ve.reciprocal(v_sp(11), v_sp(10))
                        ve.drain()
                        for bt in range(bs + 1):
                            ve.tensor_tensor(v_sp(4 + bt), v_sp(4 + bt), v_sp(11),
                                             op=ALU.mult)
                        ve.drain()
                        for bt in range(bs + 1):
                            for h in range(16):
                                g = h // 4
                                if bt == 0:
                                    ins = ve.tensor_scalar(
                                        v_acc(h * 128, 128), v_vtm(bt, g),
                                        v_sp(4 + bt)[:, h:h + 1], None, op0=ALU.mult)
                                else:
                                    ins = ve.scalar_tensor_tensor(
                                        v_acc(h * 128, 128), v_vtm(bt, g),
                                        v_sp(4 + bt)[:, h:h + 1], v_acc(h * 128, 128),
                                        op0=ALU.mult, op1=ALU.add)
                            if bt < bs:
                                ve.drain()
                        C.inc("sV", ins)

                    @blk.scalar
                    def _(sc):
                        C.wait(sc, "sV")
                        C.inc("sS", sc.copy(v_accb(), v_acc()))

                    for h in range(16):
                        @blk.tensor
                        def _(te, h=h):
                            if h <= 1:
                                C.wait(te, "sS", "sV")
                            else:
                                C.wait(te, "sS" if h % 2 == 0 else "sV")
                            C.inc("sT", te.transpose(PTR[h % 2][:, :],
                                                     v_accb(h * 128, 128), identb[:, :]))

                        if h % 2 == 0:
                            @blk.scalar
                            def _(sc, h=h, t0=t0):
                                C.wait(sc, "sT")
                                C.inc("sS", sc.copy(v_ao2(h, t0, 128), PTR[0][:, :]))
                        else:
                            @blk.vector
                            def _(ve, h=h, t0=t0):
                                C.wait(ve, "sT")
                                C.inc("sV", ve.tensor_copy(v_ao2(h, t0, 128),
                                                           PTR[1][:, :]))

        if DBG == "p2b":
            with nc.Block() as blk:
                @blk.sync
                def _(sy):
                    C.wait(sy, "sS", "sV")
                    C.inc("dO", sy.dma_start(dbg["dao2"][:, :],
                                             view(AR_D, BF16, 0, 128, 0, 16 * NT)))
                    sy.wait_ge(C.sem["dO"], C.cnt["dO"])
            return nc

        def v_xmoe(c, t0=0, nn=NT):
            return view(AR_C, BF16, 0, 128, (c * NT + t0) * 2, nn)

        gfm = view(AR_C, F32, 0, 8, 32 * 1024, NT)

        with nc.Block() as blk, \
             nc.psum_tensor("p_mc", [128, 1024], F32) as p_mc, \
             nc.psum_tensor("p_md", [128, 1024], F32) as p_md:
            prj_ev = []
            prj_te = []
            WPB = [C.cnt["dW0"], C.cnt["dW1"]]
            for nq in range(16):
                @blk.gpsimd
                def _(gp, nq=nq):
                    if nq < 2:
                        C.wait(gp, "sT")
                    else:
                        gp.wait_ge(C.sem["sT"], prj_te[nq - 2])
                    C.inc("dW%d" % (nq % 2), gp.dma_start(
                        view(AR_A, BF16, 0, 128, 32 * 1024 + (nq % 2) * 4096, 2048),
                        I["wproj"][nq].rearrange("p a b -> p (a b)"),
                        max_dma_last_dim=DMA_CAST_MAX))

                @blk.tensor
                def _(te, nq=nq):
                    te.wait_ge(C.sem["dW%d" % (nq % 2)],
                               WPB[nq % 2] + 16 * (nq // 2 + 1))
                    if nq == 0:
                        C.wait(te, "sS", "sV")
                    elif nq >= 2:
                        te.wait_ge(C.sem["sV"], prj_ev[nq - 2])
                    pm = p_mc if nq % 2 == 0 else p_md
                    for tau in range(2):
                        for kc in range(16):
                            ins = te.matmul(pm[:, tau * 512:(tau + 1) * 512],
                                            v_wslab(nq % 2, kc * 128, 128),
                                            v_ao2(kc, tau * 512, 512),
                                            start=(kc == 0), stop=(kc == 15))
                    prj_te.append(C.inc("sT", ins))

                @blk.vector
                def _(ve, nq=nq):
                    C.wait(ve, "sT")
                    pm = p_mc if nq % 2 == 0 else p_md
                    ve.tensor_tensor(v_xn32(nq, 0, 512), pm[:, 0:512],
                                     v_xn32(nq, 0, 512), op=ALU.add)
                    prj_ev.append(C.inc("sV", ve.tensor_tensor(
                        v_xn32(nq, 512, 512), pm[:, 512:1024],
                        v_xn32(nq, 512, 512), op=ALU.add)))

        def v_x2(par, nn=NT):
            return view(AR_A, BF16, 0, 128, 32 * 1024 + par * 2048, nn)

        def v_s2b(par, nn=NT):
            return view(AR_A, BF16, 0, 128, 36 * 1024 + par * 2048, nn)

        with nc.Block() as blk, \
             nc.psum_tensor("p_l1", [1, 1024], F32) as p_l1, \
             nc.psum_tensor("p_l2", [1, 1024], F32) as p_l2:
            for c in range(16):
                @blk.vector
                def _(ve, c=c):
                    C.wait(ve, "sT")
                    C.inc("sV", ve.tensor_copy(v_x2(c % 2), v_xn32(c)))

                @blk.scalar
                def _(sc, c=c):
                    C.wait(sc, "sT")
                    C.inc("sS", sc.activation(v_s2b(c % 2), v_xn32(c), AF.Square))

                @blk.tensor
                def _(te, c=c):
                    C.wait(te, "sV", "sS")
                    for half in range(2):
                        te.matmul(p_l1[:, half * 512:(half + 1) * 512], onesb[:, :],
                                  view(AR_A, BF16, 0, 128,
                                       32 * 1024 + (c % 2) * 2048 + half * 1024, 512),
                                  start=(c == 0), stop=(c == 15))
                        ins = te.matmul(p_l2[:, half * 512:(half + 1) * 512], onesb[:, :],
                                        view(AR_A, BF16, 0, 128,
                                             36 * 1024 + (c % 2) * 2048 + half * 1024, 512),
                                        start=(c == 0), stop=(c == 15))
                    C.inc("sT", ins)

            for qt in range(4):
                cs = slice(qt * 256, (qt + 1) * 256)

                @blk.vector
                def _(ve, qt=qt, cs=cs):
                    C.wait(ve, "sT", "sS")
                    ve.drain()
                    ve.tensor_scalar(R[0:1, 1024:1280], p_l1[:, cs], 1.0 / D, None,
                                     op0=ALU.mult)
                    ve.drain()
                    ve.tensor_tensor(R[0:1, 1280:1536], R[0:1, 1024:1280],
                                     R[0:1, 1024:1280], op=ALU.mult)
                    ve.drain()
                    C.inc("sV", ve.scalar_tensor_tensor(
                        R[0:1, 1280:1536], p_l2[:, cs], 1.0 / D, R[0:1, 1280:1536],
                        op0=ALU.mult, op1=ALU.subtract))

                @blk.scalar
                def _(sc, qt=qt, cs=cs):
                    C.wait(sc, "sV")
                    sc.activation(R2[0:1, cs], R[0:1, 1280:1536], AF.Ln,
                                  bias=epsb[0:1, :])
                    sc.drain()
                    C.inc("sS", sc.activation(R2[0:1, cs], R2[0:1, cs],
                                              AF.Exp, scale=-0.5))

                @blk.vector
                def _(ve, qt=qt, cs=cs):
                    C.wait(ve, "sS")
                    C.inc("sV", ve.scalar_tensor_tensor(
                        R2[0:1, 1024 + qt * 256:1024 + (qt + 1) * 256],
                        R[0:1, 1024:1280], -1.0, R2[0:1, cs],
                        op0=ALU.mult, op1=ALU.mult))

        with nc.Block() as blk, \
             nc.psum_tensor("p_r2", [128, 1024], F32) as p_r2, \
             nc.psum_tensor("p_m2", [128, 1024], F32) as p_m2:
            @blk.tensor
            def _(te):
                C.wait(te, "sV", "sS")
                te.matmul(p_r2[:, 0:512], onesf[:, :], R2[0:1, 0:512], start=True, stop=True)
                te.matmul(p_r2[:, 512:1024], onesf[:, :], R2[0:1, 512:1024], start=True, stop=True)
                te.matmul(p_m2[:, 0:512], onesf[:, :], R2[0:1, 1024:1536], start=True, stop=True)
                C.inc("sT", te.matmul(p_m2[:, 512:1024], onesf[:, :], R2[0:1, 1536:2048],
                                      start=True, stop=True))

            @blk.vector
            def _(ve):
                C.wait(ve, "sT")
                for c in range(16):
                    ve.tensor_tensor(v_lscr(), v_xn32(c), p_r2[:, :], op=ALU.mult)
                    ve.drain()
                    ve.tensor_tensor(v_lscr(), v_lscr(), p_m2[:, :], op=ALU.add)
                    ve.drain()
                    ins = ve.tensor_scalar(v_xmoe(c), v_lscr(), l2w[:, c:c + 1],
                                           l2b[:, c:c + 1], op0=ALU.mult, op1=ALU.add)
                    if c == 15:
                        C.inc("sV", ins)
                    else:
                        ve.drain()

        def v_g8(i, nn=8):
            return view(AR_A, F32, 0, 128, 40 * 1024 + i * 32, nn)

        with nc.Block() as blk, \
             nc.psum_tensor("p_rt", [128, 8], F32) as p_rt, \
             nc.psum_tensor("p_gt", [8, 128], F32) as p_gt:
            for tc in range(8):
                @blk.tensor
                def _(te, tc=tc):
                    C.wait(te, "sV", "dW")
                    for kc in range(15):
                        te.matmul(p_rt[:, :], v_xmoe(kc, tc * 128, 128),
                                  wroutb[:, kc * 8:kc * 8 + 8], start=(kc == 0), stop=False)
                    C.inc("sT", te.matmul(p_rt[:, :], v_xmoe(15, tc * 128, 128),
                                          wroutb[:, 120:128], start=False, stop=True))

                @blk.vector
                def _(ve, tc=tc):
                    C.wait(ve, "sT")
                    ve.reduce_max(v_g8(20, 1), p_rt[:, :], axis=AX.X, negate=True)
                    C.inc("sV", ve.tensor_copy(v_g8(0), p_rt[:, :]))

                @blk.scalar
                def _(sc, tc=tc):
                    C.wait(sc, "sV")
                    C.inc("sS", sc.activation(v_g8(4), v_g8(0), AF.Exp,
                                              bias=v_g8(20, 1)))

                @blk.vector
                def _(ve, tc=tc):
                    C.wait(ve, "sS")
                    ve.reduce_max(v_g8(21, 1), v_g8(4), axis=AX.X)
                    ve.drain()
                    ve.tensor_scalar(v_g8(8), v_g8(4), v_g8(21, 1), None, op0=ALU.is_equal)
                    ve.drain()
                    ve.scalar_tensor_tensor(v_g8(12), v_g8(8), NEG, v_g8(4),
                                            op0=ALU.mult, op1=ALU.add)
                    ve.drain()
                    ve.reduce_max(v_g8(22, 1), v_g8(12), axis=AX.X)
                    ve.drain()
                    ve.tensor_tensor(v_g8(23, 1), v_g8(21, 1), v_g8(22, 1), op=ALU.add)
                    ve.drain()
                    ve.reciprocal(v_g8(24, 1), v_g8(23, 1))
                    ve.drain()
                    ve.tensor_scalar(v_g8(8), v_g8(4), v_g8(22, 1), None, op0=ALU.is_ge)
                    ve.drain()
                    ve.tensor_tensor(v_g8(8), v_g8(4), v_g8(8), op=ALU.mult)
                    ve.drain()
                    C.inc("sV", ve.tensor_scalar(v_g8(8), v_g8(8), v_g8(24, 1), None,
                                                 op0=ALU.mult))

                @blk.tensor
                def _(te, tc=tc):
                    C.wait(te, "sV")
                    C.inc("sT", te.transpose(p_gt[:, :], v_g8(8), identf[:, :]))

                @blk.scalar
                def _(sc, tc=tc):
                    C.wait(sc, "sT")
                    C.inc("sS", sc.copy(gfm[:, tc * 128:(tc + 1) * 128], p_gt[:, :]))

        if DBG == "p2c":
            with nc.Block() as blk:
                @blk.sync
                def _(sy):
                    C.wait(sy, "sS", "sV")
                    C.inc("dO", sy.dma_start(dbg["dh2"][:, :],
                                             view(AR_B, F32, 0, 128, 0, 16 * NT)))
                    C.inc("dO", sy.dma_start(dbg["dxmoe"][:, :],
                                             view(AR_C, BF16, 0, 128, 0, 16 * NT)))
                    C.inc("dO", sy.dma_start(dbg["dgfm"][:, :], gfm))
                    sy.wait_ge(C.sem["dO"], C.cnt["dO"])
            return nc

        def v_htil(n, c0=0, nn=512):
            return view(AR_A, BF16, 0, 128, (n * 512 + c0) * 2, nn)

        def v_w1b(par, kc):
            return view(AR_A, BF16, 0, 128,
                        32 * 1024 + par * 4096 + kc * 256, 128)

        def v_w2b(par, ncb):
            return view(AR_D, BF16, 0, 128, par * 8192 + ncb * 256, 128)

        def v_gt3(par):
            return view(AR_D, BF16, 0, 128, 16 * 1024 + par * 1024, 512)

        gbv = view(AR_D, F32, 0, 128, 18 * 1024, 512)
        oneh8s = view(AR_D, F32, 0, 8, 20 * 1024, 1024)

        with nc.Block() as blk, \
             nc.psum_tensor("p_ha", [128, 512], F32) as p_ha, \
             nc.psum_tensor("p_hb", [128, 512], F32) as p_hb, \
             nc.psum_tensor("p_hc", [128, 512], F32) as p_hc, \
             nc.psum_tensor("p_hd", [128, 512], F32) as p_hd, \
             nc.psum_tensor("p_ya", [128, 512], F32) as p_ya, \
             nc.psum_tensor("p_yb", [128, 512], F32) as p_yb, \
             nc.psum_tensor("p_yc", [128, 512], F32) as p_yc, \
             nc.psum_tensor("p_yd", [128, 512], F32) as p_yd:
            PH = [[p_ha, p_hb], [p_hc, p_hd]]
            PY = [[p_ya, p_yb], [p_yc, p_yd]]

            def v_gt3(tau, par):
                return view(AR_D, BF16, 0, 128, 16 * 1024 + (tau * 2 + par) * 1024, 512)

            def v_gbv(tau):
                return view(AR_D, F32, 0, 128, 20 * 1024 + tau * 2048, 512)

            oneh8s2 = view(AR_D, F32, 0, 8, 24 * 1024, 1024)

            def v_htil2(tau, n, c0=0, nn=512):
                return view(AR_A, BF16, 0, 128, (tau * 16 + n) * 1024 + c0 * 2, nn)

            @blk.sync
            def _(sy):
                C.inc("dA", sy.dma_start(oneh8s2, I["oneh8"][:, :]))

            for e in range(NE):
                for tau in range(2):
                    @blk.tensor
                    def _(te, e=e, tau=tau):
                        C.wait(te, "dA", "sS", "sV")
                        C.inc("sT", te.matmul(p_ya[:, :],
                                              oneh8s2[:, e * 128:(e + 1) * 128],
                                              gfm[:, tau * 512:(tau + 1) * 512],
                                              start=True, stop=True))

                    @blk.vector
                    def _(ve, tau=tau):
                        C.wait(ve, "sT")
                        C.inc("sV", ve.tensor_copy(v_gbv(tau), p_ya[:, :]))

                for half in range(2):
                    n0 = half * 16
                    T_h = C.cnt["sT"]
                    S_h = C.cnt["sS"]
                    V_h = C.cnt["sV"]
                    WB = [C.cnt["dW0"], C.cnt["dW1"], C.cnt["dW2"]]

                    @blk.gpsimd
                    def _(gp, e=e, half=half, n0=n0, T_h=T_h):
                        for n in range(16):
                            if n >= 3:
                                wv = T_h + n - 2
                            elif e == 0 and half == 0:
                                wv = T_h
                            else:
                                wv = T_h - 3 + n
                            gp.wait_ge(C.sem["sT"], wv)
                            C.inc("dW%d" % (n % 3), gp.dma_start(
                                view(AR_A, BF16, 0, 128, 32 * 1024 + (n % 3) * 4096,
                                     2048),
                                I["w1"][e, n0 + n].rearrange("p a b -> p (a b)"),
                                max_dma_last_dim=DMA_CAST_MAX))

                    @blk.scalar
                    def _(sc, T_h=T_h, V_h=V_h):
                        for n in range(16):
                            sc.wait_ge(C.sem["sT"], T_h + n + 1)
                            if n >= 2:
                                sc.wait_ge(C.sem["sV"], V_h + n - 1)
                            sc.activation(v_gt3(0, n % 2), PH[0][n % 2][:, :], AF.Gelu)
                            C.inc("sS", sc.activation(v_gt3(1, n % 2),
                                                      PH[1][n % 2][:, :], AF.Gelu))

                    for n in range(16):
                        @blk.tensor
                        def _(te, n=n, WB=WB, S_h=S_h):
                            te.wait_ge(C.sem["dW%d" % (n % 3)],
                                       WB[n % 3] + 16 * (n // 3 + 1))
                            if n >= 2:
                                te.wait_ge(C.sem["sS"], S_h + n - 1)
                            for kc in range(16):
                                for tau in range(2):
                                    ins = te.matmul(PH[tau][n % 2][:, :],
                                                    v_w1b(n % 3, kc),
                                                    v_xmoe(kc, tau * 512, 512),
                                                    start=(kc == 0), stop=(kc == 15))
                            C.inc("sT", ins)

                        @blk.vector
                        def _(ve, n=n, S_h=S_h):
                            ve.wait_ge(C.sem["sS"], S_h + n + 1)
                            if n == 0:
                                ve.drain()
                            ve.tensor_tensor(v_htil2(0, n), v_gt3(0, n % 2),
                                             v_gbv(0), op=ALU.mult)
                            C.inc("sV", ve.tensor_tensor(v_htil2(1, n),
                                                         v_gt3(1, n % 2),
                                                         v_gbv(1), op=ALU.mult))

                    T2 = C.cnt["sT"]
                    V2 = C.cnt["sV"]
                    WB2 = [C.cnt["dW3"], C.cnt["dW4"]]

                    @blk.gpsimd
                    def _(gp, e=e, half=half, T2=T2):
                        for dc in range(16):
                            gp.wait_ge(C.sem["sT"],
                                       T2 + dc - 1 if dc >= 2 else T2 - 3 + dc)
                            C.inc("dW%d" % (dc % 2 + 3), gp.dma_start(
                                view(AR_D, BF16, 0, 128, (dc % 2) * 4096, 2048),
                                I["w2"][e, half, dc].rearrange("p a b -> p (a b)"),
                                max_dma_last_dim=DMA_CAST_MAX))

                    for dc in range(16):
                        @blk.tensor
                        def _(te, dc=dc, WB2=WB2, V2=V2):
                            te.wait_ge(C.sem["dW%d" % (dc % 2 + 3)],
                                       WB2[dc % 2] + 16 * (dc // 2 + 1))
                            te.wait_ge(C.sem["sV"],
                                       V2 + dc - 1 if dc >= 2 else V2)
                            for nc_ in range(16):
                                for tau in range(2):
                                    ins = te.matmul(
                                        PY[tau][dc % 2][:, :],
                                        view(AR_D, BF16, 0, 128,
                                             (dc % 2) * 4096 + nc_ * 256, 128),
                                        v_htil2(tau, nc_),
                                        start=(nc_ == 0), stop=(nc_ == 15))
                            C.inc("sT", ins)

                        @blk.vector
                        def _(ve, dc=dc, T2=T2):
                            ve.wait_ge(C.sem["sT"], T2 + dc + 1)
                            ve.tensor_tensor(
                                v_xn32(dc, 0, 512), PY[0][dc % 2][:, :],
                                v_xn32(dc, 0, 512), op=ALU.add)
                            C.inc("sV", ve.tensor_tensor(
                                v_xn32(dc, 512, 512), PY[1][dc % 2][:, :],
                                v_xn32(dc, 512, 512), op=ALU.add))

            @blk.scalar
            def _(sc):
                C.wait(sc, "sV")
                for c in range(15):
                    sc.activation(v_xn32(c), v_xn32(c), AF.Silu)
                C.inc("sS", sc.activation(v_xn32(15), v_xn32(15), AF.Silu))

            @blk.sync
            def _(sy):
                C.wait(sy, "sS")
                for c in range(16):
                    C.inc("dA", sy.dma_start(s_g[c], v_xn32(c)))

        if DBG == "p3":
            with nc.Block() as blk:
                @blk.sync
                def _(sy):
                    C.wait(sy, "dA")
                    for c in range(16):
                        C.inc("dO", sy.dma_start(dbg["dg"][c], s_g[c]))
                    sy.wait_ge(C.sem["dO"], C.cnt["dO"])
            return nc

        def v_xn16r(c, t0=0, nn=NT):
            return view(AR_A, BF16, 0, 128, (c * NT + t0) * 2, nn)

        def v_cosq(p0, np_, c0, nn):
            return view(AR_A, F32, p0, np_, 32 * 1024 + c0 * 4, nn)

        def v_sinq(p0, np_, c0, nn):
            return view(AR_A, F32, p0, np_, 36 * 1024 + c0 * 4, nn)

        def v_mask(qt, nn=640):
            return view(AR_D, BF16, 0, 128, 18 * 1024 + qt * 1280, nn)

        def v_qrb(p0, np_, c0, nn):
            return view(AR_A, BF16, p0, np_, 40 * 1024 + c0 * 2, nn)

        def v_dfm(c0=0, nn=640):
            return view(AR_D, BF16, 0, 128, 21 * 1024 + c0 * 2, nn)

        def v_ao(h, t0=0, nn=NT):
            return view(AR_B, BF16, 0, 128, (h * NT + t0) * 2, nn)

        SCR = 32 * 1024
        def v_sb(i, p0=0, np_=128, c0=0, nn=640):
            return view(AR_B, F32, p0, np_, SCR + i * 2560 + c0 * 4, nn)

        def v_wq4(par, c0, nn):
            return view(AR_B, BF16, 0, 128, 48 * 1024 + par * 4096 + c0 * 2, nn)

        def v_gt4(par, nn=NT):
            return view(AR_B, F32, 0, 128, 56 * 1024 + par * 4096, nn)

        def v_xt4(par, nn=NT):
            return view(AR_D, F32, 0, 128, par * 4096, nn)

        def v_ob(par, nn=NT):
            return view(AR_D, F32, 0, 128, 8 * 1024 + par * 4096, nn)

        def v_row(i, par=0):
            return view(AR_D, F32, 0, 128, 16 * 1024 + par * 32 + i * 4, 1)

        def v_sbp(idx, par, nn=640):
            if par == 0:
                return v_sb(idx if idx != 2 else 5, 0, 128, 0, nn)
            return view(AR_D, F32, 0, 128, 23040 + idx * 2560, nn)

        def v_sbpb(idx, par, nn=640):
            if par == 0:
                return view(AR_B, BF16, 0, 128, SCR + idx * 2560, nn)
            return view(AR_D, BF16, 0, 128, 23040 + idx * 2560, nn)

        def v_eb(idx, par, nn=640):
            return view(AR_B, BF16, 0, 128, SCR + (2 + idx) * 2560 + par * 1280, nn)

        with nc.Block() as blk:
            @blk.sync
            def _(sy):
                C.wait(sy, "dA")
                for b in range(4):
                    C.inc("dA", sy.dma_start(
                        view(AR_A, BF16, 0, 128, b * 512, 0 + 4096)
                        .rearrange("p (c u) -> p c u", u=1024)[:, :, 0:256] if False
                        else view(AR_A, BF16, 0, 128, 0, 16 * NT)
                        .rearrange("p (c u) -> p c u", u=NT)[:, :, b * 256:(b + 1) * 256],
                        s_xn16[b].rearrange("p (c u) -> p c u", u=256)))
                C.inc("dA", sy.dma_start(view(AR_C, BF16, 0, 128, 0, 4 * 3072), s_k[:, :]))
                C.inc("dA", sy.dma_start(view(AR_C, BF16, 0, 128, 24 * 1024, 24 * 512),
                                         s_v[:, :]))
                C.inc("dA", sy.dma_start(v_cosq(0, 128, 0, NT), I["cqh"][:, :]))
                C.inc("dA", sy.dma_start(v_sinq(0, 128, 0, NT), I["sqh"][:, :]))
                C.inc("dA", sy.dma_start(v_sb(0), I["cql"][:, 0:640]))
                C.inc("dA", sy.dma_start(v_sb(0, 0, 128, 640, 384), I["cql"][:, 640:1024]))
                C.inc("dA", sy.dma_start(v_sb(2), I["sql"][:, 0:640]))
                C.inc("dA", sy.dma_start(v_sb(2, 0, 128, 640, 384), I["sql"][:, 640:1024]))


            @blk.gpsimd
            def _(gp):
                C.inc("dW", gp.dma_start(v_mask(0), I["mask"][0],
                                         max_dma_last_dim=DMA_CAST_MAX))
                C.inc("dW", gp.dma_start(v_mask(1), I["mask"][1],
                                         max_dma_last_dim=DMA_CAST_MAX))

            @blk.vector
            def _(ve):
                C.wait(ve, "dA", "dW")
                ve.tensor_tensor(v_cosq(0, 128, 0, NT), v_cosq(0, 128, 0, NT),
                                 view(AR_B, F32, 0, 128, SCR, NT), op=ALU.add)
                C.inc("sV", ve.tensor_tensor(v_sinq(0, 128, 0, NT),
                                             v_sinq(0, 128, 0, NT),
                                             view(AR_B, F32, 0, 128, SCR + 2 * 2560, NT),
                                             op=ALU.add))

        with nc.Block() as blk, \
             nc.psum_tensor("p_s1h", [128, 640], F32) as p_s1h, \
             nc.psum_tensor("p_s2h", [128, 640], F32) as p_s2h, \
             nc.psum_tensor("p_t4a", [128, 128], BF16) as p_t4a, \
             nc.psum_tensor("p_t4b", [128, 128], BF16) as p_t4b, \
             nc.psum_tensor("p_av", [128, 128], F32) as p_av:
            PT4 = [p_t4a, p_t4b]
            sT_ = C.sem["sT"]; sV_ = C.sem["sV"]; sS_ = C.sem["sS"]
            prevm = None
            for h in range(16):
                g = h // 4

                @blk.gpsimd
                def _(gp, h=h):
                    C.wait(gp, "sT")
                    C.inc("dW", gp.dma_start(
                        view(AR_B, BF16, 0, 128, 48 * 1024 + (h % 2) * 4096, 2048),
                        I["wq"][h].rearrange("p a b -> p (a b)"),
                        max_dma_last_dim=DMA_CAST_MAX))
                    C.inc("dW", gp.dma_start(
                        view(AR_B, BF16, 0, 128, 56 * 1024 + (h % 2) * 4096, 2048),
                        I["wqp"][h].rearrange("p a b -> p (a b)"),
                        max_dma_last_dim=DMA_CAST_MAX))

                rope_v = {}
                for tau in range(2):
                    @blk.tensor
                    def _(te, h=h, tau=tau, prevm=prevm, rope_v=rope_v):
                        C.wait(te, "dW")
                        if prevm is None:
                            C.wait(te, "sV", "sS")
                        elif tau == 0:
                            te.wait_ge(sV_, prevm["b17"])
                        else:
                            te.wait_ge(sV_, rope_v[0])
                        for kc in range(16):
                            te.matmul(p_s1h[:, 0:512], v_wq4(h % 2, kc * 128, 128),
                                      v_xn16r(kc, tau * 512, 512),
                                      start=(kc == 0), stop=(kc == 15))
                        for kc in range(15):
                            te.matmul(p_s2h[:, 0:512],
                                      view(AR_B, BF16, 0, 128,
                                           56 * 1024 + (h % 2) * 4096 + kc * 256, 128),
                                      v_xn16r(kc, tau * 512, 512),
                                      start=(kc == 0), stop=False)
                        C.inc("sT", te.matmul(
                            p_s2h[:, 0:512],
                            view(AR_B, BF16, 0, 128,
                                 56 * 1024 + (h % 2) * 4096 + 15 * 256, 128),
                            v_xn16r(15, tau * 512, 512), start=False, stop=True))

                    @blk.vector
                    def _(ve, tau=tau, prevm=prevm):
                        ve.wait_ge(sT_, C.cnt["sT"])
                        if prevm is not None and tau == 0:
                            ve.wait_ge(C.sem["sS"], prevm["e6"])
                        cs = tau * 512
                        ve.tensor_tensor(v_sb(0, 0, 128, 0, 512), p_s1h[:, 0:512],
                                         v_cosq(0, 128, cs, 512), op=ALU.mult)
                        ve.tensor_tensor(v_sb(1, 0, 128, 0, 512), p_s2h[:, 0:512],
                                         v_sinq(0, 128, cs, 512), op=ALU.mult)
                        ve.drain()
                        C.inc("sV", ve.tensor_tensor(
                            v_qrb(0, 128, cs, 512), v_sb(0, 0, 128, 0, 512),
                            v_sb(1, 0, 128, 0, 512), op=ALU.add))
                    rope_v[tau] = C.cnt["sV"]

                T0 = C.cnt["sT"]; V0 = C.cnt["sV"]; S0 = C.cnt["sS"]
                order_T, order_V, order_S = [], [], []
                order_T += [("A", 0), ("A", 1), ("A", 2)]
                for i in range(5):
                    order_T += [("G", i), ("A", i + 3)]
                order_T += [("G", 5), ("G", 6), ("G", 7)]
                order_V += [("B", 0), ("B", 1)]
                for i in range(6):
                    order_V += [("D", i), ("F", i), ("B", i + 2)]
                order_V += [("D", 6), ("F", 6), ("D", 7), ("F", 7)]
                order_S += [("C", 0), ("E", 0), ("C", 1)]
                for i in range(2, 8):
                    order_S += [("H", i - 2), ("E", i - 1), ("C", i)]
                order_S += [("H", 6), ("E", 7), ("H", 7)]
                tT, tV, tS = {}, {}, {}
                cnt = T0
                for it in order_T:
                    cnt += 1 if it[0] == "A" else 6
                    tT[it] = cnt
                cnt = V0
                for it in order_V:
                    cnt += 1
                    tV[it] = cnt
                cnt = S0
                for it in order_S:
                    cnt += 1 if it[0] in ("C", "E") else 6
                    tS[it] = cnt

                def bump(name, instr, val):
                    instr.then_inc(C.sem[name], 1)
                    assert val == C.cnt[name] + 1, (name, val, C.cnt[name])
                    C.cnt[name] = val

                @blk.scalar
                def _(sc, h=h):
                    for it in order_S:
                        kind, i = it
                        par = i % 2
                        if kind == "C":
                            sc.wait_ge(sV_, tV[("B", i)])
                            if i >= 2:
                                sc.wait_ge(sV_, tV[("F", i - 2)])
                            sc.activation(v_eb(0, par), v_sbp(0, par), AF.Exp,
                                          bias=v_row(0, par), accum_out=v_row(2, par))
                            bump("sS", sc.activation(v_eb(1, par), v_sbp(1, par),
                                                     AF.Exp, bias=v_row(1, par),
                                                     accum_out=v_row(3, par)), tS[it])
                        elif kind == "E":
                            sc.wait_ge(sV_, tV[("D", i)])
                            bump("sS", sc.activation(
                                v_sbp(2, par),
                                view(AR_B, BF16, 0, 128, SCR + 4 * 2560, 640),
                                AF.Relu, scale=-1.0, accum_out=v_row(6)), tS[it])
                        else:
                            base = tS[it] - 6
                            for j in range(5):
                                sc.wait_ge(sT_, tT[("G", i)] - 6 + 1 + j)
                                base += 1
                                bump("sS", sc.copy(
                                    view(AR_D, BF16, 0, 128, 30720 + j * 256, 128),
                                    PT4[j % 2][:, :]), base)
                            sc.wait_ge(sT_, tT[("G", i)])
                            qc = (i // 2) * 256 + (i % 2) * 128
                            bump("sS", sc.copy(v_ao(h, qc, 128), p_av[:, :]), tS[it])

                @blk.vector
                def _(ve, h=h):
                    for it in order_V:
                        kind, i = it
                        par = i % 2
                        if kind == "B":
                            qt = i % 2
                            ve.wait_ge(sT_, tT[("A", i)])
                            if i >= 2:
                                ve.wait_ge(sS_, tS[("C", i - 2)])
                            ve.scalar_tensor_tensor(v_sbp(0, par), p_s1h[:, :], 1.0,
                                                    v_mask(qt), op0=ALU.mult,
                                                    op1=ALU.add)
                            ve.scalar_tensor_tensor(v_sbp(1, par), p_s2h[:, :], 1.0,
                                                    v_mask(qt), op0=ALU.mult,
                                                    op1=ALU.add)
                            ve.drain()
                            ve.reduce_max(v_row(0, par), v_sbp(0, par), axis=AX.X,
                                          negate=True)
                            bump("sV", ve.reduce_max(v_row(1, par), v_sbp(1, par),
                                                     axis=AX.X, negate=True), tV[it])
                        elif kind == "D":
                            ve.wait_ge(sS_, tS[("C", i)])
                            ve.reciprocal(v_row(4), v_row(3, par))
                            ve.drain()
                            ve.scalar_tensor_tensor(v_row(5), v_row(2, par),
                                                    lamb[:, h:h + 1], v_row(4),
                                                    op0=ALU.mult, op1=ALU.mult)
                            ve.drain()
                            bump("sV", ve.scalar_tensor_tensor(
                                view(AR_B, BF16, 0, 128, SCR + 4 * 2560, 640),
                                v_eb(1, par), v_row(5), v_eb(0, par),
                                op0=ALU.mult, op1=ALU.subtract), tV[it])
                        else:
                            ve.wait_ge(sS_, tS[("E", i)])
                            if i > 0:
                                ve.wait_ge(sT_, tT[("G", i - 1)] - 1)
                            ve.scalar_tensor_tensor(v_row(7, par), v_row(2, par),
                                                    1e-6, v_row(6), op0=ALU.mult,
                                                    op1=ALU.add)
                            ve.drain()
                            ve.reciprocal(v_row(7, par), v_row(7, par))
                            ve.drain()
                            bump("sV", ve.tensor_scalar(v_dfm(), v_sbp(2, par),
                                                        v_row(7, par), None,
                                                        op0=ALU.mult), tV[it])

                for it in order_T:
                    kind, i = it
                    if kind == "A":
                        b, qt = i // 2, i % 2
                        qc = b * 256 + qt * 128
                        w0 = b * 768 + qt * 128

                        @blk.tensor
                        def _(te, g=g, qc=qc, w0=w0, i=i, it=it):
                            te.wait_ge(sV_, tV[("B", i - 1)] if i > 0 else V0)
                            te.matmul(p_s1h[:, 0:512], v_qrb(0, 64, qc, 128),
                                      v_ks(g, w0, 512, 0, 64), start=True, stop=True)
                            te.matmul(p_s1h[:, 512:640], v_qrb(0, 64, qc, 128),
                                      v_ks(g, w0 + 512, 128, 0, 64),
                                      start=True, stop=True)
                            te.matmul(p_s2h[:, 0:512], v_qrb(64, 64, qc, 128),
                                      v_ks(g, w0, 512, 64, 64), start=True, stop=True)
                            bump("sT", te.matmul(p_s2h[:, 512:640],
                                                 v_qrb(64, 64, qc, 128),
                                                 v_ks(g, w0 + 512, 128, 64, 64),
                                                 start=True, stop=True), tT[it])
                    else:
                        b2, qt2 = i // 2, i % 2

                        @blk.tensor
                        def _(te, g=g, i=i, b2=b2, qt2=qt2, it=it):
                            base = tT[it] - 6
                            te.wait_ge(sV_, tV[("F", i)])
                            for j in range(5):
                                if j >= 2:
                                    te.wait_ge(sS_, tS[("H", i)] - 6 + 1 + (j - 2))
                                elif i > 0:
                                    te.wait_ge(sS_, tS[("H", i - 1)] - (1 if j == 0
                                                                       else 2))
                                base += 1
                                bump("sT", te.transpose(
                                    PT4[j % 2][:, :],
                                    view(AR_D, BF16, 0, 128, 21 * 1024 + j * 256, 128),
                                    identb[:, :]), base)
                            te.wait_ge(sS_, tS[("H", i)] - 1)
                            if i > 0:
                                te.wait_ge(sS_, tS[("H", i - 1)])
                            for j in range(4):
                                te.matmul(p_av[:, :],
                                          v_vs(b2 * 6 + qt2 + j, g * 128, 128),
                                          view(AR_D, BF16, 0, 128, 30720 + j * 256, 128),
                                          start=(j == 0), stop=False)
                            bump("sT", te.matmul(
                                p_av[:, :], v_vs(b2 * 6 + qt2 + 4, g * 128, 128),
                                view(AR_D, BF16, 0, 128, 30720 + 4 * 256, 128),
                                start=False, stop=True), tT[it])

                prevm = {"b17": tV[("B", 7)], "e6": tS[("C", 6)]}

        with nc.Block() as blk, \
             nc.psum_tensor("p_oa", [128, 1024], F32) as p_oa, \
             nc.psum_tensor("p_ob", [128, 1024], F32) as p_ob:
            PO = [p_oa, p_ob]
            wo_te = []
            wo_ev = []
            wo_st = []
            WOB = [C.cnt["dW0"], C.cnt["dW1"]]
            for nq in range(16):
                @blk.gpsimd
                def _(gp, nq=nq):
                    if nq < 2:
                        C.wait(gp, "sT")
                    else:
                        gp.wait_ge(C.sem["sT"], wo_te[nq - 2])
                    C.inc("dW%d" % (nq % 2), gp.dma_start(
                        view(AR_B, BF16, 0, 128, 48 * 1024 + (nq % 2) * 4096, 2048),
                        I["wo"][nq].rearrange("p a b -> p (a b)"),
                        max_dma_last_dim=DMA_CAST_MAX))

                @blk.sync
                def _(sy, nq=nq):
                    if nq < 2:
                        C.wait(sy, "sV")
                    else:
                        sy.wait_ge(C.sem["sV"], wo_ev[nq - 2])
                    C.inc("dA", sy.dma_start(v_gt4(nq % 2), s_g[nq]))
                    C.inc("dA", sy.dma_start(v_xt4(nq % 2), I["x"][nq]))

                @blk.tensor
                def _(te, nq=nq):
                    te.wait_ge(C.sem["dW%d" % (nq % 2)],
                               WOB[nq % 2] + 16 * (nq // 2 + 1))
                    if nq < 2:
                        C.wait(te, "sS", "sV")
                    else:
                        te.wait_ge(C.sem["sV"], wo_ev[nq - 2])
                    for tau in range(2):
                        for kc in range(16):
                            ins = te.matmul(PO[nq % 2][:, tau * 512:(tau + 1) * 512],
                                            v_wq4(nq % 2, kc * 128, 128),
                                            v_ao(kc, tau * 512, 512),
                                            start=(kc == 0), stop=(kc == 15))
                    wo_te.append(C.inc("sT", ins))

                @blk.vector
                def _(ve, nq=nq):
                    C.wait(ve, "sT", "dA")
                    if nq < 2:
                        C.wait(ve, "dO")
                    else:
                        ve.wait_ge(C.sem["dO%d" % (nq % 2)], wo_st[nq - 2])
                    ve.tensor_tensor(v_ob(nq % 2), PO[nq % 2][:, :], v_gt4(nq % 2),
                                     op=ALU.mult)
                    ve.drain()
                    wo_ev.append(C.inc("sV", ve.tensor_tensor(v_ob(nq % 2),
                                                              v_ob(nq % 2),
                                                              v_xt4(nq % 2),
                                                              op=ALU.add)))

                @blk.sync
                def _(sy, nq=nq):
                    C.wait(sy, "sV")
                    wo_st.append(C.inc("dO%d" % (nq % 2),
                                       sy.dma_start(out[nq], v_ob(nq % 2))))

            @blk.sync
            def _(sy):
                sy.wait_ge(C.sem["dO"], C.cnt["dO"])
                sy.wait_ge(C.sem["dO0"], C.cnt["dO0"])
                sy.wait_ge(C.sem["dO1"], C.cnt["dO1"])

    return nc



def kernel(**inputs):
    x = np.asarray(inputs["L0_attn_x"], dtype=np.float32)
    shared = _prep_shared(inputs)
    in_maps = [_prep_core(c, x, shared) for c in range(NCORES)]
    nc = build_nc()
    res = run_bass_kernel_spmd(nc, in_maps, core_ids=list(range(NCORES)))
    outs = []
    for c in range(NCORES):
        o = res.results[c]["out"]
        outs.append(o.reshape(D, B, SL).transpose(1, 2, 0))
    return np.concatenate(outs, axis=1).astype(np.float32)
